# version 11
# baseline (speedup 1.0000x reference)
import contextlib
import numpy as np
import ml_dtypes
import concourse.bass as bass
import concourse.mybir as mybir
from concourse.bass_utils import run_bass_kernel_spmd

dt = mybir.dt
ALU = mybir.AluOpType
AF = mybir.ActivationFunctionType
AX = mybir.AxisListType.X
BF16 = ml_dtypes.bfloat16
F32 = dt.float32
BF = dt.bfloat16
I32 = dt.int32

B, S, H = 2, 8192, 2048
HEADS, HD = 8, 256
FF = 8192
L = 2
NC = 8
TC = 2048
CH = 512
NP = TC // CH
KC = H // 128
FC = FF // 128
E, TOPK, IFF, GSZ = 16, 4, 1536, 128
EPS = 1e-5
EPER = E // NC
SHC = IFF // NC

GAMMA = 1.0 - np.exp(np.linspace(np.log(1.0 / 32), np.log(1.0 / 512), HEADS))



WHOLE = (0, 1 << 60)


def _norm_raw(keys):
    out = []
    for k in keys:
        if isinstance(k, tuple):
            out.append(k)
        else:
            out.append((k, 0, 1 << 60))
    return out


class Sched:
    EXES = ("pe", "act", "dve", "pool", "sp")

    def __init__(self):
        self.items = []
        self.cnt = {e: 0 for e in self.EXES}
        self.wr = {}
        self.rd = {}
        self.hw = {}
        self.pending = []
        self.vinfo = {}
        self.ncc = 0

    def _norm(self, keys):
        out = []
        for name, lo, hi in _norm_raw(keys):
            vi = self.vinfo.get(name)
            if vi is None:
                out.append((name, lo, hi))
            else:
                off, cols, es = vi
                out.append(("AR", off + lo * es, off + min(hi, cols) * es))
        return out

    def _dep(self, waits, trk, cnt):
        if cnt <= 0:
            return
        waits[trk] = max(waits.get(trk, 0), cnt)

    def emit(self, exe, fn, reads=(), writes=(), dma=False, inc=1, cc=False,
             skey=None):
        assert exe in self.EXES
        reads = self._norm(reads)
        writes = self._norm(writes)
        if dma:
            assert skey is not None
            k = self._norm([skey])[0]
            trk, amt = ("D",) + k, 16
        elif cc:
            trk, amt = ("C", self.ncc), 1
            self.ncc += 1
        else:
            trk, amt = exe, inc
        if trk not in self.cnt:
            self.cnt[trk] = 0
        waits = {}
        if dma and self.cnt[trk]:
            waits[trk] = self.cnt[trk]
        for name, lo, hi in reads:
            for rec in self.wr.get(name, ()):
                if rec[0] < hi and lo < rec[1]:
                    self._dep(waits, rec[2], rec[3])
        for name, lo, hi in writes:
            for rec in self.wr.get(name, ()):
                if rec[0] < hi and lo < rec[1]:
                    self._dep(waits, rec[2], rec[3])
            for rec in self.rd.get(name, ()):
                if rec[0] < hi and lo < rec[1]:
                    self._dep(waits, rec[2], rec[3])
        item = dict(exe=exe, fn=fn, waits=waits, trk=trk, amt=(amt if inc else 0),
                    reads=reads, writes=writes)
        self.items.append(item)
        if inc:
            self.cnt[trk] += amt
            cdone = self.cnt[trk]
            for it2 in self.pending:
                self._record(it2, trk, cdone)
            self.pending.clear()
            self._record(item, trk, cdone)
        else:
            assert trk == "pe", "inc=0 only supported on pe"
            self.pending.append(item)
        return item

    def _record(self, item, trk, cdone):
        for name, lo, hi in item["writes"]:
            lst = self.wr.setdefault(name, [])
            lst[:] = [r for r in lst if not (lo <= r[0] and r[1] <= hi)]
            lst.append([lo, hi, trk, cdone])
        for name, lo, hi in item["reads"]:
            lst = self.rd.setdefault(name, [])
            lst[:] = [r for r in lst if not (lo <= r[0] and r[1] <= hi
                                             and r[2] == trk)]
            lst.append([lo, hi, trk, cdone])

    def wait_only(self, exe, reads=()):
        self.emit(exe, None, reads=reads, inc=1)

    def barrier(self):
        assert not any(self.pending.values())
        snap = dict(self.cnt)
        self.items.append(dict(exe=None, barrier=snap))

    def prepare(self):
        floor = {e: {} for e in self.EXES}
        per = {e: [] for e in self.EXES}
        for it in self.items:
            if it["exe"] is None:
                for e in self.EXES:
                    for t, c in it["barrier"].items():
                        if c > 0:
                            floor[e][t] = max(floor[e].get(t, 0), c)
                continue
            e = it["exe"]
            w = dict(it["waits"])
            for t, c in floor[e].items():
                w[t] = max(w.get(t, 0), c)
            floor[e] = {}
            per[e].append((w, it))
        return per

    def replay_engine(self, e, eng, sems, per):
        hw = self.hw.setdefault(e, {})
        for w, it in per[e]:
            for t, c in sorted(w.items(), key=str):
                if t == e and t == "pe":
                    continue
                if c > hw.get(t, 0):
                    eng.wait_ge(sems[t], c)
                    hw[t] = c
            if it["fn"] is not None:
                inst = it["fn"](eng)
            elif it["amt"]:
                inst = eng.nop()
            else:
                continue
            if it["amt"]:
                inst.then_inc(sems[it["trk"]],
                              it["amt"] if it["amt"] > 1 else 1)



def _tile_w(w):
    K, N = w.shape
    kc, nch = K // 128, N // 128
    t = w.reshape(kc, 128, nch, 128).transpose(2, 1, 0, 3)
    return np.ascontiguousarray(t.reshape(nch, 128, kc * 128))


def _colmaj(v, nch):
    return np.ascontiguousarray(v.reshape(nch, 128).T)


def prep_inputs(inputs):
    x = np.asarray(inputs["L0_attn_x"])
    maps = []

    shared = {}
    for nm, key in (("wq", "L0_attn_Wq"), ("wk", "L0_attn_Wk"),
                    ("wv", "L0_attn_Wv"), ("wg", "L0_attn_Wg"),
                    ("wo", "L0_attn_Wo")):
        w = np.asarray(inputs[key])
        shared[nm] = np.stack([_tile_w(w[l]) for l in range(L)])
    w1 = np.asarray(inputs["L0_attn_W1"])
    shared["w1"] = np.stack([_tile_w(w1[l]) for l in range(L)])
    w2 = np.asarray(inputs["L0_attn_W2"])
    shared["w2"] = np.stack([_tile_w(w2[l]) for l in range(L)])
    shared["lng"] = _colmaj(np.asarray(inputs["L0_attn_ln_gamma"]), KC)
    shared["lnb"] = _colmaj(np.asarray(inputs["L0_attn_ln_beta"]), KC)
    b1 = np.asarray(inputs["L0_attn_b1"])
    shared["b1c"] = np.concatenate([_colmaj(b1[l], FC) for l in range(L)], 1)
    b2 = np.asarray(inputs["L0_attn_b2"])
    shared["b2c"] = np.concatenate([_colmaj(b2[l], KC) for l in range(L)], 1)
    gg = np.asarray(inputs["L0_attn_gn_gamma"])
    gb = np.asarray(inputs["L0_attn_gn_beta"])
    shared["gng"] = np.ascontiguousarray(
        np.broadcast_to(gg[:, None, :], (L, 128, H)))
    shared["gnb"] = np.ascontiguousarray(
        np.broadcast_to(gb[:, None, :], (L, 128, H)))
    gw = np.asarray(inputs["L0_ffn_gate_w"])
    shared["gwt"] = np.ascontiguousarray(
        gw.T.reshape(KC, 128, E).transpose(1, 0, 2).reshape(128, KC * E))

    a = np.arange(CH, dtype=np.float64)
    dtm = np.zeros((HEADS, NP, 128, CH), np.float64)
    for h in range(HEADS):
        for bc in range(NP):
            bpos = bc * 128 + np.arange(128, dtype=np.float64)
            diff = a[None, :] - bpos[:, None]
            dtm[h, bc] = np.where(diff >= 0, GAMMA[h] ** np.maximum(diff, 0), 0.0)
    shared["dtm"] = dtm.astype(BF16)
    gq = np.stack([GAMMA[h] ** (a + 1.0) for h in range(HEADS)])
    shared["gqt"] = np.ascontiguousarray(
        np.broadcast_to(gq.reshape(1, HEADS * CH), (128, HEADS * CH))).astype(BF16)
    kr = np.zeros((128, HEADS * NP), np.float64)
    for h in range(HEADS):
        for mc in range(NP):
            kr[:, h * NP + mc] = GAMMA[h] ** (CH - 1 - (mc * 128 + np.arange(128)))
    shared["krev"] = kr.astype(np.float32)
    ident = np.eye(128, dtype=np.float32)
    shared["ident"] = ident.astype(BF16)
    shared["identf"] = ident
    cst = np.zeros((128, 4), np.float32)
    cst[:, 0] = EPS
    shared["cstf"] = cst
    pm8 = np.arange(128) % 8
    shared["m15c"] = (np.int32(15) << (4 * pm8).astype(np.int32)).reshape(128, 1).astype(np.int32)
    shared["p2nc"] = (2.0 ** (-4.0 * pm8)).reshape(128, 1).astype(np.float32)

    gup = np.asarray(inputs["L0_ffn_gate_up_packed"])
    gus = np.asarray(inputs["L0_ffn_gate_up_scales"])
    dnp = np.asarray(inputs["L0_ffn_down_packed"])
    dns = np.asarray(inputs["L0_ffn_down_scales"])
    sgup = np.asarray(inputs["L0_ffn_shared_gate_up_packed"])
    sgus = np.asarray(inputs["L0_ffn_shared_gate_up_scales"])
    sdnp = np.asarray(inputs["L0_ffn_shared_down_packed"])
    sdns = np.asarray(inputs["L0_ffn_shared_down_scales"])

    def rep_words(p):
        return np.repeat(p, 8, axis=0)

    def rep_scales(s):
        return np.repeat(s, GSZ, axis=0)

    for c in range(NC):
        m = dict(shared)
        b, s0 = c // 4, (c % 4) * TC
        m["x0"] = np.ascontiguousarray(x[b, s0:s0 + TC])

        wcc = np.zeros((128, NC * KC), np.float64)
        for cp in range(NC):
            if cp // 4 == b and cp < c:
                for ch_i in range(KC):
                    h = ch_i // 2
                    wcc[:, cp * KC + ch_i] = GAMMA[h] ** (TC * (c - cp - 1))
        m["wcc"] = wcc.astype(np.float32)

        es = np.zeros((128, EPER * E), np.float32)
        for e in range(EPER):
            es[:, e * E + c * EPER + e] = 1.0
        m["esel"] = es
        e0 = c * EPER
        m["gupw"] = np.ascontiguousarray(
            np.stack([rep_words(gup[e0 + i]) for i in range(EPER)]))
        m["gusr"] = np.ascontiguousarray(
            np.stack([rep_scales(gus[e0 + i]) for i in range(EPER)]))
        m["dnpw"] = np.ascontiguousarray(
            np.stack([rep_words(dnp[e0 + i]) for i in range(EPER)]))
        m["dnsr"] = np.ascontiguousarray(
            np.stack([rep_scales(dns[e0 + i]) for i in range(EPER)]))
        j0 = c * SHC
        cols = np.concatenate([np.arange(j0, j0 + SHC),
                               np.arange(IFF + j0, IFF + j0 + SHC)])
        m["sgupw"] = np.ascontiguousarray(rep_words(sgup)[:, cols])
        m["sgusr"] = np.ascontiguousarray(rep_scales(sgus)[:, cols])
        m["sdnpw"] = np.ascontiguousarray(rep_words(sdnp[j0 // 8:(j0 + SHC) // 8]))
        m["sdnsr"] = np.ascontiguousarray(rep_scales(sdns)[j0:j0 + SHC])
        maps.append(m)
    return maps


INPUT_SPECS = {
    "x0": ((TC, H), F32),
    "wq": ((L, KC, 128, H), F32), "wk": ((L, KC, 128, H), F32),
    "wv": ((L, KC, 128, H), F32), "wg": ((L, KC, 128, H), F32),
    "wo": ((L, KC, 128, H), F32),
    "w1": ((L, FC, 128, H), F32), "w2": ((L, KC, 128, FF), F32),
    "lng": ((128, KC), F32), "lnb": ((128, KC), F32),
    "b1c": ((128, L * FC), F32), "b2c": ((128, L * KC), F32),
    "gng": ((L, 128, H), F32), "gnb": ((L, 128, H), F32),
    "gwt": ((128, KC * E), F32),
    "dtm": ((HEADS, NP, 128, CH), BF), "gqt": ((128, HEADS * CH), BF),
    "krev": ((128, HEADS * NP), F32), "ident": ((128, 128), BF),
    "m15c": ((128, 1), I32), "p2nc": ((128, 1), F32),
    "esel": ((128, EPER * E), F32), "identf": ((128, 128), F32),
    "cstf": ((128, 4), F32),
    "wcc": ((128, NC * KC), F32),
    "gupw": ((EPER, H, 2 * IFF), I32), "gusr": ((EPER, H, 2 * IFF), F32),
    "dnpw": ((EPER, IFF, H), I32), "dnsr": ((EPER, IFF, H), F32),
    "sgupw": ((H, 2 * SHC), I32), "sgusr": ((H, 2 * SHC), F32),
    "sdnpw": ((SHC, H), I32), "sdnsr": ((SHC, H), F32),
}



class KB:

    def __init__(self, taps=()):
        self.taps = tuple(taps)
        self.nc = bass.Bass("TRN2", target_bir_lowering=False, debug=False)
        self.S = Sched()
        self.t = {}
        self.sb = {}
        self.ps = {}
        self._stack = contextlib.ExitStack()
        self._declare_dram()

    def _declare_dram(self):
        nc, t = self.nc, self.t
        for name, (shape, dty) in INPUT_SPECS.items():
            t[name] = nc.dram_tensor(name, list(shape), dty, kind="ExternalInput").ap()
        t["yout"] = nc.dram_tensor("yout", [TC, H], F32, kind="ExternalOutput").ap()
        for nm, sh in (("wqb", (L, KC, 128, H)), ("wkb", (L, KC, 128, H)),
                       ("wvb", (L, KC, 128, H)), ("wgb", (L, KC, 128, H)),
                       ("wob", (L, KC, 128, H)), ("w1b", (L, FC, 128, H)),
                       ("w2b", (L, KC, 128, FF))):
            t[nm] = nc.dram_tensor(nm, list(sh), BF, kind="Internal").ap()
        t["zd"] = nc.dram_tensor("zd", [TC, H], F32, kind="Internal").ap()
        t["x1d"] = nc.dram_tensor("x1d", [TC, H], F32, kind="Internal").ap()
        t["qsp"] = nc.dram_tensor("qsp", [H, TC], BF, kind="Internal").ap()
        t["gsp"] = nc.dram_tensor("gsp", [H, TC], BF, kind="Internal").ap()
        t["ysp"] = nc.dram_tensor("ysp", [TC, H], BF, kind="Internal").ap()
        t["vin"] = nc.dram_tensor("vin", [KC * 128, HD], F32, kind="Internal").ap()
        t["vout"] = nc.dram_tensor("vout", [NC * KC * 128, HD], F32, kind="Internal").ap()
        t["xfin"] = nc.dram_tensor("xfin", [H, TC], BF, kind="Internal").ap()
        t["xfout"] = nc.dram_tensor("xfout", [NC * H, TC], BF, kind="Internal").ap()
        t["cmbin"] = nc.dram_tensor("cmbin", [TC, E], F32, kind="Internal").ap()
        t["cmbout"] = nc.dram_tensor("cmbout", [B * S, E], F32, kind="Internal").ap()
        t["part"] = nc.dram_tensor("part", [B * S, H], BF, kind="Internal").ap()
        t["rsout"] = nc.dram_tensor("rsout", [TC, H], BF, kind="Internal").ap()
        for tp in self.taps:
            t["tap_" + tp] = nc.dram_tensor("tap_" + tp, [TC, H], F32,
                                            kind="ExternalOutput").ap()

    ARENA_BYTES = 212736

    def make_arena(self):
        self._ar = self._stack.enter_context(
            self.nc.sbuf_tensor("AR", [128, self.ARENA_BYTES // 2], BF))
        self._ar_bf = self._ar.ap()
        self._ar_f32 = self._ar.ap().bitcast(F32)
        self._ar_i32 = self._ar.ap().bitcast(I32)

    def view(self, name, cursor, cols, dty):
        es = 2 if dty == BF else 4
        off = (cursor + es - 1) // es * es
        ar = {BF: self._ar_bf, F32: self._ar_f32, I32: self._ar_i32}[dty]
        self.sb[name] = ar[:, off // es: off // es + cols]
        self.S.vinfo[name] = (off, cols, es)
        end = off + cols * es
        assert end <= self.ARENA_BYTES, f"arena overflow at {name}: {end}"
        return end

    def alloc(self, name, shape, dty, psum=False):
        assert psum
        h = self._stack.enter_context(self.nc.psum_tensor(shape, dty))
        self.ps[name] = h
        return h

    def dma(self, eng, out, in_, reads=(), writes=(), skey=None):
        self.S.emit(eng, lambda e, o=out, i=in_: e.dma_start(out=o, in_=i),
                    reads=reads, writes=writes, dma=True, skey=skey)

    def tap(self, name, src):
        if name not in self.taps:
            return
        dst = self.t["tap_" + name]
        buf = self.sb["zt"]
        for tc in range(TC // 128):
            r = (tc % 2) * H
            self.dma("sp", buf[:, r:r + H], src[tc * 128:(tc + 1) * 128, :],
                     reads=[(name + "_src", tc, tc + 1)],
                     writes=[("zt", r, r + H)])
            self.dma("sp", dst[tc * 128:(tc + 1) * 128, :], buf[:, r:r + H],
                     reads=[("zt", r, r + H)], writes=[("tap", tc, tc + 1)])
        self.S.barrier()

    def precast(self):
        t = self.t
        plan = []
        for l in range(L):
            for j in range(KC):
                plan.append(("wg", "wgb", l, j))
            for j in range(KC):
                plan += [("wq", "wqb", l, j), ("wk", "wkb", l, j),
                         ("wv", "wvb", l, j)]
            for j in range(KC):
                plan.append(("wo", "wob", l, j))
            for j in range(FC):
                plan.append(("w1", "w1b", l, j))
            for j in range(KC):
                plan.append(("w2", "w2b", l, j))
        for i, (src, dst, l, j) in enumerate(plan):
            self.dma("pool", t[dst][l, j], t[src][l, j],
                     writes=[(dst, l * 64 + j, l * 64 + j + 1)],
                     skey=("pc%d" % (i % 4), 0, 1))

    def load_consts(self):
        t = self.t
        c = 0
        c = self.view("ident", c, 128, BF)
        c = self.view("identf", c, 128, F32)
        c = self.view("cstf", c, 4, F32)
        c = self.view("lng", c, KC, F32)
        c = self.view("lnb", c, KC, F32)
        c = self.view("b1c", c, L * FC, F32)
        c = self.view("b2c", c, L * KC, F32)
        c = self.view("gwt", c, KC * E, F32)
        c = self.view("m15c", c, 1, I32)
        c = self.view("p2nc", c, 1, F32)
        c = self.view("esel", c, EPER * E, F32)
        c = self.view("zt", c, 2 * H, F32)
        c = self.view("xnbf", c, 2 * H, BF)
        c = self.view("st", c, 64, F32)
        c = self.view("cw", c, 64, F32)
        c = self.view("m8", c, 8, F32)
        c = self.view("mk", c, 16, F32)
        c = self.view("cmb", c, 16, F32)
        c = self.view("sq", c, H, F32)
        c = self.view("sqg", c, HD, F32)
        c = self.view("ev", c, 2 * CH, BF)
        self._glob_end = c
        self.nc.const_aps.register(F32, float(EPS), self.sb["cstf"][:, 0:1])
        for nm in ("ident", "identf", "lng", "lnb", "b1c", "b2c", "gwt", "m15c",
                   "p2nc", "esel", "cstf"):
            self.dma("sp", self.sb[nm][:, :], t[nm], writes=[nm],
                     skey=("cload", 0, 1))

    def z_init(self):
        t, sb, S = self.t, self.sb, self.S
        for tc in range(TC // 128):
            self._ln_chunk(src=t["x0"][tc * 128:(tc + 1) * 128, :], tc=tc,
                           affine=False, out_dram=t["zd"], out_key="zd")

    def _ln_chunk(self, src, tc, affine, out_dram, out_key, pre_add=None):
        sb, S = self.sb, self.S
        r = (tc % 2) * H
        zt, st = sb["zt"], sb["st"]
        self.dma("sp", zt[:, r:r + H], src, writes=[("zt", r, r + H)],
                 skey=("zt", r, r + H))
        if pre_add is not None:
            r2 = ((tc + 1) % 2) * H
            self.dma("sp", zt[:, r2:r2 + H], pre_add, writes=[("zt", r2, r2 + H)],
                     skey=("zt", r2, r2 + H))
            S.emit("dve", lambda v, a=zt[:, r:r + H], b=zt[:, r2:r2 + H]:
                   v.tensor_add(a, a, b),
                   reads=[("zt", r, r + H), ("zt", r2, r2 + H)],
                   writes=[("zt", r, r + H)])
        self._ln_stats_apply(zt, r, tc, affine=affine)
        self.dma("sp", out_dram[tc * 128:(tc + 1) * 128, :], zt[:, r:r + H],
                 reads=[("zt", r, r + H)], writes=[(out_key, tc, tc + 1)],
                 skey=("zt", r, r + H))

    def _ln_stats_apply(self, zt, r, tag, affine):
        sb, S = self.sb, self.S
        st, sq = sb["st"], sb["sq"]
        so = (tag % 2) * 8
        ztk = ("zt", r, r + H)
        stk = ("st", so, so + 8)
        sqk = ("sq", 0, H)
        S.emit("dve", lambda v: v.reduce_sum(st[:, so:so + 1], zt[:, r:r + H],
                                             axis=AX, negate=True),
               reads=[ztk], writes=[stk])
        S.emit("act", lambda a: a.activation(sq[:], zt[:, r:r + H], AF.Square,
                                             accum_out=st[:, so + 1:so + 2]),
               reads=[ztk], writes=[stk, sqk])
        S.emit("dve", lambda v: v.tensor_scalar(st[:, so:so + 1], st[:, so:so + 1],
                                                1.0 / H, None, ALU.mult),
               reads=[stk], writes=[stk])
        S.emit("dve", lambda v: v.tensor_tensor(st[:, so + 2:so + 3],
                                                st[:, so:so + 1], st[:, so:so + 1],
                                                ALU.mult),
               reads=[stk], writes=[stk])
        S.emit("dve", lambda v: v.scalar_tensor_tensor(st[:, so + 2:so + 3],
                                                       st[:, so + 1:so + 2],
                                                       1.0 / H, st[:, so + 2:so + 3],
                                                       ALU.mult, ALU.subtract),
               reads=[stk], writes=[stk])
        S.emit("act", lambda a: a.activation(st[:, so + 2:so + 3],
                                             st[:, so + 2:so + 3], AF.Sqrt,
                                             bias=float(EPS)),
               reads=[stk, "cstf"], writes=[stk])
        S.emit("dve", lambda v: v.reciprocal(st[:, so + 3:so + 4],
                                             st[:, so + 2:so + 3]),
               reads=[stk], writes=[stk])
        S.emit("dve", lambda v: v.tensor_scalar(zt[:, r:r + H], zt[:, r:r + H],
                                                st[:, so:so + 1], st[:, so + 3:so + 4],
                                                ALU.add, ALU.mult),
               reads=[ztk, stk], writes=[ztk])

    def alloc_attn(self):
        c = self._glob_end
        c = self.view("xnT", c, KC * CH, BF)
        c = self.view("wsl", c, 2 * H, BF)
        c = self.view("qh", c, 2 * CH, BF)
        c = self.view("kh", c, 2 * CH, BF)
        c = self.view("vfm", c, 2 * CH, BF)
        c = self.view("qsh", c, 2 * CH, BF)
        c = self.view("yh", c, NP * HD, BF)
        c = self.view("gh", c, 2 * CH, BF)
        yf_at = c
        c = self.view("vtm", c, NP * HD, BF)
        c = self.view("ktm", c, NP * HD, BF)
        c = self.view("ktms", c, NP * HD, BF)
        c = self.view("sdth", c, NP * CH, BF)
        c = self.view("dts", c, 4 * CH, BF)
        c = self.view("gevb", c, 2 * CH, BF)
        assert c - yf_at == HEADS * NP * HD * 2
        self.view("yfull", yf_at, HEADS * NP * HD, BF)
        c = self.view("sst", c, KC * HD, F32)
        c = self.view("sbf", c, KC * HD, BF)
        c = self.view("vtl", c, 2 * HD, F32)
        h1g_at = c
        c = self.view("h1g", c, 16 * CH, BF)
        self.view("vtlB0", h1g_at, KC * HD, F32)
        self.view("xnT2", h1g_at, KC * CH, BF)
        retaF_at = c
        c = self.view("retaF", c, NP * H, F32)
        self.view("vtlB1", retaF_at, KC * HD, F32)
        cb = retaF_at
        for nm, cols in (("qhB", 2 * CH), ("khB", 2 * CH), ("vfmB", 2 * CH),
                         ("qshB", 2 * CH), ("vtmB", NP * HD), ("ktmB", NP * HD),
                         ("ktmsB", NP * HD), ("sdthB", NP * CH), ("dtsB", 4 * CH)):
            cb = self.view(nm, cb, cols, BF)
        assert cb <= c
        c = self.view("gqt", c, HEADS * CH, BF)
        c = self.view("krev", c, HEADS * NP, F32)
        c = self.view("wcc", c, NC * KC, F32)
        c = self.view("gngl", c, H, F32)
        c = self.view("gnbl", c, H, F32)
        c = self.view("xftS", c, KC * 128, BF)
        c = self.view("xft32", c, KC * 128, F32)
        self._attn_end = c
        for nm in ("gqt", "krev", "wcc"):
            self.dma("sp", self.sb[nm][:, :], self.t[nm], writes=[nm],
                     skey=("cload", 0, 1))
        self.alloc("pp0", [128, 512], F32, psum=True)
        self.alloc("pp1", [128, 512], F32, psum=True)
        self.alloc("py0", [128, 512], F32, psum=True)
        self.alloc("py1", [128, 512], F32, psum=True)
        self.alloc("pt0", [128, 1024], BF, psum=True)
        self.alloc("pt1", [128, 1024], BF, psum=True)
        self.alloc("pl0", [128, 512], F32, psum=True)
        self.alloc("pl1", [128, 512], F32, psum=True)

    def _mm_acc(self, psum, pkey, lhs_fn, rhs_fn, n, reads):
        S = self.S
        for i in range(n):
            lf, rf = lhs_fn(i), rhs_fn(i)
            S.emit("pe",
                   lambda e, a=lf, b=rf, st=(i == 0), sp=(i == n - 1), ps=psum:
                   e.matmul(ps, a, b, start=st, stop=sp),
                   reads=reads(i), writes=[pkey], inc=(1 if i == n - 1 else 0))

    def _transpose(self, out_psum, pkey, in_ap, reads, f32=False):
        ident = self.sb["identf" if f32 else "ident"]
        ikey = "identf" if f32 else "ident"
        self.S.emit("pe", lambda e, o=out_psum, i=in_ap, idt=ident:
                    e.transpose(o, i, idt[:, :]),
                    reads=list(reads) + [ikey], writes=[pkey])

    def _slab(self, which, l, j, rot, cols=None, src_cols=None):
        wsl = self.sb["wsl"]
        w = self.t[which][l, j]
        if src_cols is not None:
            w = w[:, src_cols[0]:src_cols[1]]
        n = cols if cols is not None else w.shape[-1]
        dst = wsl[:, rot * H: rot * H + n]
        self.dma("sp", dst, w, reads=[(which, l * 64 + j, l * 64 + j + 1)],
                 writes=[("wsl", rot * H, rot * H + n)],
                 skey=("wsl", rot * H, rot * H + 1))
        return wsl, rot * H

    def attn_phase1(self, l, p):
        nc, S, sb, t = self.nc, self.S, self.sb, self.t
        xname = "xnT" if p % 2 == 0 else "xnT2"
        xnT, wsl = sb[xname], sb["wsl"]
        pp = [self.ps["pp0"], self.ps["pp1"]]
        py = [self.ps["py0"], self.ps["py1"]]
        pt = [self.ps["pt0"], self.ps["pt1"]]
        ppk = ["pp0", "pp1"]
        pyk = ["py0", "py1"]
        ptk = ["pt0", "pt1"]

        if p > 0:
            for ch in range(KC):
                sl = slice(ch * HD, (ch + 1) * HD)
                S.emit("dve", lambda v, sl=sl: v.tensor_copy(
                    sb["sbf"][:, sl], sb["sst"][:, sl]),
                    reads=[("sst", ch * HD, (ch + 1) * HD)],
                    writes=[("sbf", ch * HD, (ch + 1) * HD)])

        for tc4 in range(4):
            tcg = p * 4 + tc4
            r = (tc4 % 2) * H
            zt, xnbf = sb["zt"], sb["xnbf"]
            if l == 0:
                self.dma("sp", zt[:, r:r + H], t["x0"][tcg * 128:(tcg + 1) * 128, :],
                         writes=[("zt", r, r + H)], skey=("zt", r, r + H))
                self._ln_stats_apply(zt, r, tc4, affine=False)
                self.dma("sp", t["zd"][tcg * 128:(tcg + 1) * 128, :], zt[:, r:r + H],
                         reads=[("zt", r, r + H)], writes=[("zd", tcg, tcg + 1)],
                         skey=("zt", r, r + H))
            else:
                self.dma("sp", zt[:, r:r + H], t["zd"][tcg * 128:(tcg + 1) * 128, :],
                         reads=[("zd", tcg, tcg + 1)], writes=[("zt", r, r + H)],
                         skey=("zt", r, r + H))
            self._ln_stats_apply(zt, r, tc4, affine=False)
            S.emit("dve", lambda v, r=r: v.tensor_copy(xnbf[:, r:r + H],
                                                       zt[:, r:r + H]),
                   reads=[("zt", r, r + H)], writes=[("xnbf", r, r + H)])
            for kc in range(KC):
                pb = (tc4 * KC + kc) % 2
                po = ((tc4 * KC + kc) // 2 % 4) * 128
                pslice = pt[pb][:, po:po + 128]
                self._transpose(pslice, ptk[pb],
                                xnbf[:, r + kc * 128: r + (kc + 1) * 128],
                                [("xnbf", r + kc * 128, r + (kc + 1) * 128)])
                dstc = kc * CH + tc4 * 128
                S.emit("dve", lambda v, ps=pslice, d=dstc, kc=kc: v.tensor_scalar(
                    xnT[:, d:d + 128], ps, sb["lng"][:, kc:kc + 1],
                    sb["lnb"][:, kc:kc + 1], ALU.mult, ALU.add),
                    reads=[ptk[pb], "lng", "lnb"],
                    writes=[(xname, dstc, dstc + 128)])

        for dc in range(KC):
            rot = dc % 2
            wslh, wo = self._slab("wgb", l, dc, rot)
            pb = dc % 2
            self._mm_acc(pp[pb][:, 0:CH], ppk[pb],
                         lambda kc, wo=wo: wsl[:, wo + kc * 128: wo + (kc + 1) * 128],
                         lambda kc: xnT[:, kc * CH:(kc + 1) * CH],
                         KC,
                         lambda kc, wo=wo: [("wsl", wo + kc * 128, wo + (kc + 1) * 128),
                                            (xname, kc * CH, (kc + 1) * CH)])
            gev = sb["gevb"]
            go = rot * CH
            S.emit("act", lambda a, pb=pb, go=go: a.activation(
                gev[:, go:go + CH], pp[pb][:, 0:CH], AF.Silu),
                reads=[ppk[pb]], writes=[("gevb", go, go + CH)])
            self.dma("act", t["gsp"][dc * 128:(dc + 1) * 128, p * CH:(p + 1) * CH],
                     gev[:, go:go + CH],
                     reads=[("gevb", go, go + CH)], writes=[("gsp", dc, dc + 1)],
                     skey=("gevb", go, go + CH))

        def sfx(h):
            return "" if h % 2 == 0 else "B"

        def prep(h):
            sx = sfx(h)
            qh, kh, vfm = sb["qh" + sx], sb["kh" + sx], sb["vfm" + sx]
            vtm, ktm, ktms, qsh = (sb["vtm" + sx], sb["ktm" + sx],
                                   sb["ktms" + sx], sb["qsh" + sx])
            self.dma("sp", sb["dts" + sx][:, :],
                     t["dtm"][h].rearrange("bc q a -> q bc a"),
                     writes=[("dts" + sx, 0, 4 * CH)], skey=("dts" + sx, 0, 1))
            for hf in range(2):
                dc = 2 * h + hf
                for wname, dst, dkey in (("wqb", qh, "qh" + sx),
                                         ("wkb", kh, "kh" + sx),
                                         ("wvb", vfm, "vfm" + sx)):
                    rot = (dc + (0 if wname != "wvb" else 1)) % 2
                    wslh, wo = self._slab(wname, l, dc, rot)
                    pb = hf
                    self._mm_acc(pp[pb][:, 0:CH], ppk[pb],
                                 lambda kc, wo=wo: wsl[:, wo + kc * 128: wo + (kc + 1) * 128],
                                 lambda kc: xnT[:, kc * CH:(kc + 1) * CH],
                                 KC,
                                 lambda kc, wo=wo: [("wsl", wo + kc * 128, wo + (kc + 1) * 128),
                                                    (xname, kc * CH, (kc + 1) * CH)])
                    S.emit("act", lambda a, d=dst, o=hf * CH, pb=pb: a.copy(
                        d[:, o:o + CH], pp[pb][:, 0:CH]),
                        reads=[ppk[pb]], writes=[(dkey, hf * CH, (hf + 1) * CH)])
            for src, skeyn, dst, dkey in ((vfm, "vfm" + sx, vtm, "vtm" + sx),
                                          (kh, "kh" + sx, ktm, "ktm" + sx)):
                for hf in range(2):
                    for tcl in range(4):
                        i = hf * 4 + tcl
                        pb = i % 2
                        po = (i // 2 % 4) * 128
                        pslice = pt[pb][:, po:po + 128]
                        self._transpose(pslice, ptk[pb],
                                        src[:, hf * CH + tcl * 128: hf * CH + (tcl + 1) * 128],
                                        [(skeyn, hf * CH + tcl * 128,
                                          hf * CH + (tcl + 1) * 128)])
                        dc0 = tcl * HD + hf * 128
                        S.emit("act", lambda a, ps=pslice, d=dc0, dst=dst: a.copy(
                            dst[:, d:d + 128], ps),
                            reads=[ptk[pb]],
                            writes=[(dkey, dc0, dc0 + 128)])
            for mc in range(NP):
                S.emit("act", lambda a, mc=mc, h=h, ktms=ktms, ktm=ktm: a.mul(
                    ktms[:, mc * HD:(mc + 1) * HD], ktm[:, mc * HD:(mc + 1) * HD],
                    sb["krev"][:, h * NP + mc: h * NP + mc + 1]),
                    reads=[("ktm" + sx, mc * HD, (mc + 1) * HD), "krev"],
                    writes=[("ktms" + sx, mc * HD, (mc + 1) * HD)])
            for hf in range(2):
                S.emit("dve", lambda v, hf=hf, h=h, qsh=qsh, qh=qh, sx=sx: v.tensor_mul(
                    qsh[:, hf * CH:(hf + 1) * CH], qh[:, hf * CH:(hf + 1) * CH],
                    sb["gqt"][:, h * CH:(h + 1) * CH]),
                    reads=[("qh" + sx, hf * CH, (hf + 1) * CH), "gqt"],
                    writes=[("qsh" + sx, hf * CH, (hf + 1) * CH)])
            self.dma("act",
                     t["qsp"][2 * h * 128:(2 * h + 2) * 128,
                              p * CH:(p + 1) * CH].rearrange(
                                  "(hf q) a -> q hf a", q=128),
                     qsh[:, :].rearrange("q (hf a) -> q hf a", a=CH),
                     reads=[("qsh" + sx, 0, 2 * CH)],
                     writes=[("qsp", 2 * h, 2 * h + 2)],
                     skey=("qsh" + sx, 0, 1))

        def rets(h):
            sx = sfx(h)
            qh, kh = sb["qh" + sx], sb["kh" + sx]
            vtm, ktms, qsh, sdth = (sb["vtm" + sx], sb["ktms" + sx],
                                    sb["qsh" + sx], sb["sdth" + sx])
            yh = sb["yh"]
            for bc in range(NP):
                pb = bc % 2
                self._mm_acc(pl[pb][:, 0:CH], plk[pb],
                             lambda hf, bc=bc: kh[:, hf * CH + bc * 128: hf * CH + bc * 128 + 128],
                             lambda hf: qh[:, hf * CH:(hf + 1) * CH],
                             2,
                             lambda hf, bc=bc: [("kh" + sx, hf * CH + bc * 128,
                                                 hf * CH + bc * 128 + 128),
                                                ("qh" + sx, hf * CH, (hf + 1) * CH)])
                S.emit("dve", lambda v, pb=pb, bc=bc, sdth=sdth, sx=sx: v.tensor_mul(
                    sdth[:, bc * CH:(bc + 1) * CH], pl[pb][:, 0:CH],
                    sb["dts" + sx][:, bc * CH:(bc + 1) * CH]),
                    reads=[plk[pb], ("dts" + sx, bc * CH, (bc + 1) * CH)],
                    writes=[("sdth" + sx, bc * CH, (bc + 1) * CH)])
            for ac in range(NP):
                pb = ac % 2
                nmm = NP + (2 if p > 0 else 0)

                def lhs(i, ac=ac, sdth=sdth, qsh=qsh):
                    if i < NP:
                        return sdth[:, i * CH + ac * 128: i * CH + ac * 128 + 128]
                    hf = i - NP
                    return qsh[:, hf * CH + ac * 128: hf * CH + ac * 128 + 128]

                def rhs(i, h=h, vtm=vtm):
                    if i < NP:
                        return vtm[:, i * HD:(i + 1) * HD]
                    hf = i - NP
                    return sb["sbf"][:, (2 * h + hf) * HD:(2 * h + hf + 1) * HD]

                def rds(i, ac=ac, h=h, sx=sx):
                    if i < NP:
                        return [("sdth" + sx, i * CH + ac * 128, i * CH + ac * 128 + 128),
                                ("vtm" + sx, i * HD, (i + 1) * HD)]
                    hf = i - NP
                    return [("qsh" + sx, hf * CH + ac * 128, hf * CH + ac * 128 + 128),
                            ("sbf", (2 * h + hf) * HD, (2 * h + hf + 1) * HD)]

                self._mm_acc(py[pb][:, 0:HD], pyk[pb], lhs, rhs, nmm, rds)
                S.emit("dve", lambda v, pb=pb, ac=ac: v.tensor_copy(
                    yh[:, ac * HD:(ac + 1) * HD], py[pb][:, 0:HD]),
                    reads=[pyk[pb]], writes=[("yh", ac * HD, (ac + 1) * HD)])
            self.dma("act",
                     t["ysp"][p * CH:(p + 1) * CH,
                              h * HD:(h + 1) * HD].rearrange(
                                  "(ac q) d -> q ac d", q=128),
                     yh[:, :].rearrange("q (ac d) -> q ac d", d=HD),
                     reads=[("yh", 0, NP * HD)],
                     writes=[("ysp", p * 8 + h, p * 8 + h + 1)],
                     skey=("yh", 0, 1))
            for hf in range(2):
                ch = 2 * h + hf
                pb = hf
                self._mm_acc(pl[pb][:, 0:HD], plk[pb],
                             lambda mc, hf=hf, ktms=ktms: ktms[:, mc * HD + hf * 128: mc * HD + hf * 128 + 128],
                             lambda mc, vtm=vtm: vtm[:, mc * HD:(mc + 1) * HD],
                             NP,
                             lambda mc, hf=hf, sx=sx: [("ktms" + sx, mc * HD + hf * 128,
                                                        mc * HD + hf * 128 + 128),
                                                       ("vtm" + sx, mc * HD, (mc + 1) * HD)])
                g512 = float(GAMMA[h] ** CH)
                S.emit("dve", lambda v, ch=ch, pb=pb, g=g512: v.scalar_tensor_tensor(
                    sb["sst"][:, ch * HD:(ch + 1) * HD],
                    sb["sst"][:, ch * HD:(ch + 1) * HD], g, pl[pb][:, 0:HD],
                    ALU.mult, ALU.add),
                    reads=[("sst", ch * HD, (ch + 1) * HD), plk[pb]],
                    writes=[("sst", ch * HD, (ch + 1) * HD)])

        pl = [self.ps["pl0"], self.ps["pl1"]]
        plk = ["pl0", "pl1"]
        prep(0)
        for h in range(HEADS):
            if h + 1 < HEADS:
                prep(h + 1)
            rets(h)

    def state_exchange(self, l):
        S, sb, t = self.S, self.sb, self.t
        self.dma("sp", t["vin"].rearrange("(ch q) d -> q ch d", q=128),
                 sb["sst"][:].rearrange("q (ch d) -> q ch d", d=HD),
                 reads=[("sst", 0, KC * HD)], writes=["vin"],
                 skey=("sst", 0, 1))
        rg = [list(range(NC))]
        S.emit("pool", lambda g: g.collective_compute(
            "AllGather", ALU.bypass, rg, [t["vin"].opt()], [t["vout"].opt()]),
            reads=["vin"], writes=["vout"], cc=True)
        for cp in range(NC):
            vn = "vtlB%d" % (cp % 2)
            vtb = sb[vn]
            self.dma("sp", vtb[:, :].rearrange("q (ch d) -> q ch d", d=HD),
                     t["vout"][cp * KC * 128:(cp + 1) * KC * 128, :].rearrange(
                         "(ch q) d -> q ch d", q=128),
                     reads=["vout"], writes=[(vn, 0, KC * HD)],
                     skey=(vn, 0, 1))
            for ch in range(KC):
                sl = slice(ch * HD, (ch + 1) * HD)
                wc = cp * KC + ch
                if cp == 0:
                    S.emit("dve", lambda v, sl=sl, wc=wc, vtb=vtb: v.tensor_scalar(
                        sb["sst"][:, sl], vtb[:, sl],
                        sb["wcc"][:, wc:wc + 1], None, ALU.mult),
                        reads=[(vn, sl.start, sl.stop), "wcc"],
                        writes=[("sst", sl.start, sl.stop)])
                else:
                    S.emit("dve", lambda v, sl=sl, wc=wc, vtb=vtb: v.scalar_tensor_tensor(
                        sb["sst"][:, sl], vtb[:, sl],
                        sb["wcc"][:, wc:wc + 1], sb["sst"][:, sl],
                        ALU.mult, ALU.add),
                        reads=[(vn, sl.start, sl.stop), "wcc",
                               ("sst", sl.start, sl.stop)],
                        writes=[("sst", sl.start, sl.stop)])

    def ph2_stage1_thunks(self, l, p):
        S, sb, t = self.S, self.sb, self.t
        pl = [self.ps["pl0"], self.ps["pl1"]]
        plk = ["pl0", "pl1"]
        thunks = []

        def sbf_th():
            for ch in range(KC):
                h = ch // 2
                gp = float(GAMMA[h] ** (CH * p))
                sl = slice(ch * HD, (ch + 1) * HD)
                S.emit("dve", lambda v, sl=sl, g=gp: v.tensor_scalar(
                    sb["sbf"][:, sl], sb["sst"][:, sl], g, None, ALU.mult),
                    reads=[("sst", sl.start, sl.stop)],
                    writes=[("sbf", sl.start, sl.stop)])
        thunks.append(sbf_th)

        def head_th(h):
            qsh, yh = sb["qsh"], sb["yh"]
            yfull = sb["yfull"]
            self.dma("sp", qsh[:, :].rearrange("q (hf a) -> q hf a", a=CH),
                     t["qsp"][2 * h * 128:(2 * h + 2) * 128,
                              p * CH:(p + 1) * CH].rearrange(
                                  "(hf q) a -> q hf a", q=128),
                     reads=[("qsp", 2 * h, 2 * h + 2)], writes=[("qsh", 0, 2 * CH)],
                     skey=("qsh", 0, 1))
            self.dma("sp", yh[:, :].rearrange("q (ac d) -> q ac d", d=HD),
                     t["ysp"][p * CH:(p + 1) * CH,
                              h * HD:(h + 1) * HD].rearrange(
                                  "(ac q) d -> q ac d", q=128),
                     reads=[("ysp", p * 8 + h, p * 8 + h + 1)],
                     writes=[("yh", 0, NP * HD)], skey=("yh", 0, 1))
            for ac in range(NP):
                pb = ac % 2
                self._mm_acc(pl[pb][:, 0:HD], plk[pb],
                             lambda hf, ac=ac: qsh[:, hf * CH + ac * 128: hf * CH + ac * 128 + 128],
                             lambda hf, h=h: sb["sbf"][:, (2 * h + hf) * HD:(2 * h + hf + 1) * HD],
                             2,
                             lambda hf, ac=ac, h=h: [
                                 ("qsh", hf * CH + ac * 128, hf * CH + ac * 128 + 128),
                                 ("sbf", (2 * h + hf) * HD, (2 * h + hf + 1) * HD)])
                yo = h * NP * HD + ac * HD
                seg = yfull[:, yo:yo + HD]
                segk = ("yfull", yo, yo + HD)
                S.emit("dve", lambda v, pb=pb, ac=ac, seg=seg: v.tensor_add(
                    seg, pl[pb][:, 0:HD], yh[:, ac * HD:(ac + 1) * HD]),
                    reads=[plk[pb], ("yh", ac * HD, (ac + 1) * HD)],
                    writes=[segk])
                st = sb["st"]
                so = (ac % 2) * 8 + 16
                stk = ("st", so, so + 8)
                S.emit("dve", lambda v, seg=seg, so=so: v.reduce_sum(
                    st[:, so:so + 1], seg, axis=AX, negate=True),
                    reads=[segk], writes=[stk])
                S.emit("act", lambda a, seg=seg, so=so: a.activation(
                    sb["sqg"][:, :], seg, AF.Square,
                    accum_out=st[:, so + 1:so + 2]),
                    reads=[segk], writes=[stk, "sqg"])
                S.emit("dve", lambda v, so=so: v.tensor_scalar(
                    st[:, so:so + 1], st[:, so:so + 1], 1.0 / HD, None, ALU.mult),
                    reads=[stk], writes=[stk])
                S.emit("dve", lambda v, so=so: v.tensor_tensor(
                    st[:, so + 2:so + 3], st[:, so:so + 1], st[:, so:so + 1],
                    ALU.mult), reads=[stk], writes=[stk])
                S.emit("dve", lambda v, so=so: v.scalar_tensor_tensor(
                    st[:, so + 2:so + 3], st[:, so + 1:so + 2], 1.0 / HD,
                    st[:, so + 2:so + 3], ALU.mult, ALU.subtract),
                    reads=[stk], writes=[stk])
                S.emit("act", lambda a, so=so: a.activation(
                    st[:, so + 2:so + 3], st[:, so + 2:so + 3], AF.Sqrt,
                    bias=float(EPS)), reads=[stk, "cstf"], writes=[stk])
                S.emit("dve", lambda v, so=so: v.reciprocal(
                    st[:, so + 3:so + 4], st[:, so + 2:so + 3]),
                    reads=[stk], writes=[stk])
                S.emit("dve", lambda v, seg=seg, so=so: v.tensor_scalar(
                    seg, seg, st[:, so:so + 1], st[:, so + 3:so + 4],
                    ALU.add, ALU.mult), reads=[segk, stk], writes=[segk])
                S.emit("dve", lambda v, seg=seg, h=h: v.tensor_mul(
                    seg, seg, sb["gngl"][:, h * HD:(h + 1) * HD]),
                    reads=[segk, "gngl"], writes=[segk])
                S.emit("dve", lambda v, seg=seg, h=h: v.tensor_add(
                    seg, seg, sb["gnbl"][:, h * HD:(h + 1) * HD]),
                    reads=[segk, "gnbl"], writes=[segk])
        for h in range(HEADS):
            thunks.append(lambda h=h: head_th(h))
        return thunks

    def attn_phase2_stage2(self, l, p, pending=()):
        pending = list(pending)
        S, sb, t = self.S, self.sb, self.t
        fmA, wsl = sb["xnT"], sb["wsl"]
        pp = [self.ps["pp0"], self.ps["pp1"]]
        ppk = ["pp0", "pp1"]
        py = [self.ps["py0"], self.ps["py1"]]
        pyk = ["py0", "py1"]
        pt = [self.ps["pt0"], self.ps["pt1"]]
        ptk = ["pt0", "pt1"]
        yfull = sb["yfull"]

        for h in range(HEADS):
            gh = sb["gh"]
            self.dma("sp", gh[:, :].rearrange("q (hf a) -> q hf a", a=CH),
                     t["gsp"][2 * h * 128:(2 * h + 2) * 128,
                              p * CH:(p + 1) * CH].rearrange(
                                  "(hf q) a -> q hf a", q=128),
                     reads=[("gsp", 2 * h, 2 * h + 2)], writes=[("gh", 0, 2 * CH)],
                     skey=("gh", 0, 1))
            for hf in range(2):
                for ac in range(NP):
                    i = hf * NP + ac
                    pb = i % 2
                    po = (i // 2 % 4) * 128
                    pslice = pt[pb][:, po:po + 128]
                    yo = h * NP * HD + ac * HD + hf * 128
                    self._transpose(pslice, ptk[pb],
                                    yfull[:, yo:yo + 128],
                                    [("yfull", yo, yo + 128)])
                    ch = 2 * h + hf
                    d = ch * CH + ac * 128
                    S.emit("dve", lambda v, ps=pslice, d=d, hf=hf, ac=ac: v.tensor_mul(
                        fmA[:, d:d + 128], ps,
                        gh[:, hf * CH + ac * 128: hf * CH + ac * 128 + 128]),
                        reads=[ptk[pb],
                               ("gh", hf * CH + ac * 128, hf * CH + ac * 128 + 128)],
                        writes=[("xnT", d, d + 128)])

        retaF = sb["retaF"]
        for tc4 in range(4):
            tcg = p * 4 + tc4
            self.dma("sp", retaF[:, tc4 * H:(tc4 + 1) * H],
                     t["zd"][tcg * 128:(tcg + 1) * 128, :],
                     reads=[("zd", tcg, tcg + 1)],
                     writes=[("retaF", tc4 * H, (tc4 + 1) * H)],
                     skey=("retaF", tc4 * H, tc4 * H + 1))
        for nc_ in range(KC):
            rot = nc_ % 2
            wslh, wo = self._slab("wob", l, nc_, rot)
            pb = nc_ % 2
            self._mm_acc(pp[pb][:, 0:CH], ppk[pb],
                         lambda kc, wo=wo: wsl[:, wo + kc * 128: wo + (kc + 1) * 128],
                         lambda kc: fmA[:, kc * CH:(kc + 1) * CH],
                         KC,
                         lambda kc, wo=wo: [("wsl", wo + kc * 128, wo + (kc + 1) * 128),
                                            ("xnT", kc * CH, (kc + 1) * CH)])
            ev = sb["ev"]
            eo = rot * CH
            S.emit("act", lambda a, pb=pb, eo=eo: a.copy(
                ev[:, eo:eo + CH], pp[pb][:, 0:CH]),
                reads=[ppk[pb]], writes=[("ev", eo, eo + CH)])
            for tc4 in range(4):
                i = nc_ * 4 + tc4
                pb2 = i % 2
                po = (i // 2 % 4) * 128
                pslice = pt[pb2][:, po:po + 128]
                self._transpose(pslice, ptk[pb2],
                                sb["ev"][:, eo + tc4 * 128: eo + (tc4 + 1) * 128],
                                [("ev", eo + tc4 * 128, eo + (tc4 + 1) * 128)])
                zcol = tc4 * H + nc_ * 128
                S.emit("dve", lambda v, ps=pslice, zc=zcol: v.tensor_add(
                    retaF[:, zc:zc + 128], retaF[:, zc:zc + 128], ps),
                    reads=[ptk[pb2], ("retaF", zcol, zcol + 128)],
                    writes=[("retaF", zcol, zcol + 128)])

        for tc4 in range(4):
            r = tc4 * H
            so = (tc4 % 2) * 8 + 32
            self._ln_stats(retaF, r, "retaF", so)
            S.emit("dve", lambda v, r=r, tc4=tc4, so=so: v.tensor_scalar(
                sb["xnbf"][:, (tc4 % 2) * H:(tc4 % 2) * H + H],
                retaF[:, r:r + H], sb["st"][:, so:so + 1],
                sb["st"][:, so + 3:so + 4], ALU.add, ALU.mult),
                reads=[("retaF", r, r + H), ("st", so, so + 8)],
                writes=[("xnbf", (tc4 % 2) * H, (tc4 % 2) * H + H)])
            for kc in range(KC):
                i = tc4 * KC + kc
                pb = i % 2
                po = (i // 2 % 4) * 128
                pslice = pt[pb][:, po:po + 128]
                rr = (tc4 % 2) * H
                self._transpose(pslice, ptk[pb],
                                sb["xnbf"][:, rr + kc * 128: rr + (kc + 1) * 128],
                                [("xnbf", rr + kc * 128, rr + (kc + 1) * 128)])
                d = kc * CH + tc4 * 128
                S.emit("dve", lambda v, ps=pslice, d=d, kc=kc: v.tensor_scalar(
                    fmA[:, d:d + 128], ps, sb["lng"][:, kc:kc + 1],
                    sb["lnb"][:, kc:kc + 1], ALU.mult, ALU.add),
                    reads=[ptk[pb], "lng", "lnb"],
                    writes=[("xnT", d, d + 128)])

        while pending:
            pending.pop(0)()
        h1g = sb["h1g"]
        for grp in range(4):
            for fs in range(KC):
                f = grp * KC + fs
                rot = fs % 2
                wslh, wo = self._slab("w1b", l, f, rot)
                pb = fs % 2
                self._mm_acc(pp[pb][:, 0:CH], ppk[pb],
                             lambda kc, wo=wo: wsl[:, wo + kc * 128: wo + (kc + 1) * 128],
                             lambda kc: fmA[:, kc * CH:(kc + 1) * CH],
                             KC,
                             lambda kc, wo=wo: [("wsl", wo + kc * 128, wo + (kc + 1) * 128),
                                                ("xnT", kc * CH, (kc + 1) * CH)])
                S.emit("act", lambda a, pb=pb, fs=fs, f=f, l=l: a.activation(
                    h1g[:, fs * CH:(fs + 1) * CH], pp[pb][:, 0:CH], AF.Gelu,
                    bias=sb["b1c"][:, l * FC + f: l * FC + f + 1]),
                    reads=[ppk[pb], "b1c"], writes=[("h1g", fs * CH, (fs + 1) * CH)])
            for nc_ in range(KC):
                rot = nc_ % 2
                w = self.t["w2b"][l, nc_][:, grp * H:(grp + 1) * H]
                dst = wsl[:, rot * H: rot * H + H]
                self.dma("sp", dst, w, reads=[("w2b", l * 64 + nc_, l * 64 + nc_ + 1)],
                         writes=[("wsl", rot * H, rot * H + H)],
                         skey=("wsl", rot * H, rot * H + 1))
                wo = rot * H
                pb = nc_ % 2
                self._mm_acc(py[pb][:, 0:CH], pyk[pb],
                             lambda fs, wo=wo: wsl[:, wo + fs * 128: wo + (fs + 1) * 128],
                             lambda fs: h1g[:, fs * CH:(fs + 1) * CH],
                             KC,
                             lambda fs, wo=wo: [("wsl", wo + fs * 128, wo + (fs + 1) * 128),
                                                ("h1g", fs * CH, (fs + 1) * CH)])
                eo = rot * CH
                if grp == 0:
                    S.emit("act", lambda a, pb=pb, eo=eo, nc_=nc_, l=l: a.add(
                        sb["ev"][:, eo:eo + CH], py[pb][:, 0:CH],
                        sb["b2c"][:, l * KC + nc_: l * KC + nc_ + 1]),
                        reads=[pyk[pb], "b2c"], writes=[("ev", eo, eo + CH)])
                else:
                    S.emit("act", lambda a, pb=pb, eo=eo: a.copy(
                        sb["ev"][:, eo:eo + CH], py[pb][:, 0:CH]),
                        reads=[pyk[pb]], writes=[("ev", eo, eo + CH)])
                for tc4 in range(4):
                    i = nc_ * 4 + tc4
                    pb2 = i % 2
                    po = (i // 2 % 4) * 128
                    pslice = pt[pb2][:, po:po + 128]
                    self._transpose(pslice, ptk[pb2],
                                    sb["ev"][:, eo + tc4 * 128: eo + (tc4 + 1) * 128],
                                    [("ev", eo + tc4 * 128, eo + (tc4 + 1) * 128)])
                    zc = tc4 * H + nc_ * 128
                    S.emit("dve", lambda v, ps=pslice, zc=zc: v.tensor_add(
                        retaF[:, zc:zc + 128], retaF[:, zc:zc + 128], ps),
                        reads=[ptk[pb2], ("retaF", zc, zc + 128)],
                        writes=[("retaF", zc, zc + 128)])
        while pending:
            pending.pop(0)()
        for tc4 in range(4):
            tcg = p * 4 + tc4
            self.dma("act", t["zd"][tcg * 128:(tcg + 1) * 128, :],
                     retaF[:, tc4 * H:(tc4 + 1) * H],
                     reads=[("retaF", tc4 * H, (tc4 + 1) * H)],
                     writes=[("zd", tcg, tcg + 1)],
                     skey=("retaF", tc4 * H, tc4 * H + 1))

    def _ln_stats(self, buf, r, key, so, n=H):
        sb, S = self.sb, self.S
        st, sq = sb["st"], sb["sq"]
        bk = (key, r, r + n)
        stk = ("st", so, so + 8)
        S.emit("dve", lambda v: v.reduce_sum(st[:, so:so + 1], buf[:, r:r + n],
                                             axis=AX, negate=True),
               reads=[bk], writes=[stk])
        S.emit("act", lambda a: a.activation(sq[:, 0:n], buf[:, r:r + n], AF.Square,
                                             accum_out=st[:, so + 1:so + 2]),
               reads=[bk], writes=[stk, ("sq", 0, n)])
        S.emit("dve", lambda v: v.tensor_scalar(st[:, so:so + 1], st[:, so:so + 1],
                                                1.0 / n, None, ALU.mult),
               reads=[stk], writes=[stk])
        S.emit("dve", lambda v: v.tensor_tensor(st[:, so + 2:so + 3],
                                                st[:, so:so + 1], st[:, so:so + 1],
                                                ALU.mult), reads=[stk], writes=[stk])
        S.emit("dve", lambda v: v.scalar_tensor_tensor(st[:, so + 2:so + 3],
                                                       st[:, so + 1:so + 2], 1.0 / n,
                                                       st[:, so + 2:so + 3],
                                                       ALU.mult, ALU.subtract),
               reads=[stk], writes=[stk])
        S.emit("act", lambda a: a.activation(st[:, so + 2:so + 3],
                                             st[:, so + 2:so + 3], AF.Sqrt,
                                             bias=float(EPS)),
               reads=[stk, "cstf"], writes=[stk])
        S.emit("dve", lambda v: v.reciprocal(st[:, so + 3:so + 4],
                                             st[:, so + 2:so + 3]),
               reads=[stk], writes=[stk])

    def attn_layer(self, l, post_pass=None):
        S, sb, t = self.S, self.sb, self.t
        self.dma("sp", sb["gngl"][:, :], t["gng"][l], writes=["gngl"],
                 skey=("gngl", 0, 1))
        self.dma("sp", sb["gnbl"][:, :], t["gnb"][l], writes=["gnbl"],
                 skey=("gnbl", 0, 1))
        S.emit("dve", lambda v: v.memset(sb["sst"][:, :], 0.0), writes=["sst"])
        for p in range(NP):
            self.attn_phase1(l, p)
        self.state_exchange(l)
        for th in self.ph2_stage1_thunks(l, 0):
            th()
        for p in range(NP):
            nxt = self.ph2_stage1_thunks(l, p + 1) if p + 1 < NP else []
            self.attn_phase2_stage2(l, p, pending=nxt)
            if post_pass is not None:
                post_pass(p)

    def alloc_moe(self):
        c = self._glob_end
        c = self.view("wguA", c, KC * 768, BF)
        c = self.view("wguB", c, KC * 768, BF)
        c = self.view("wdA", c, 3 * H, BF)
        c = self.view("wdB", c, 3 * H, BF)
        c = self.view("xft", c, KC * CH, BF)
        c = self.view("xft2", c, KC * CH, BF)
        c = self.view("act", c, 3 * CH, BF)
        c = self.view("yblk", c, 4 * H, BF)
        c = self.view("dqw", c, 2048, I32)
        c = self.view("dq1", c, 2048, I32)
        c = self.view("dq2", c, 2048, F32)
        c = self.view("dq3", c, 2048, F32)
        c = self.view("dqs", c, 2048, F32)
        c = self.view("cbc", c, 32 * 4 * EPER, F32)
        c = self.view("cbr", c, 16, F32)
        self._moe_end = c


    def x1_xf_stage(self, p=None):
        S, sb, t = self.S, self.sb, self.t
        pp = [self.ps["pp0"], self.ps["pp1"]]
        ppk = ["pp0", "pp1"]
        pt = [self.ps["pt0"], self.ps["pt1"]]
        ptk = ["pt0", "pt1"]
        pl = self.ps["pl0"]
        rng = range(TC // 128) if p is None else range(p * 4, (p + 1) * 4)
        for tc in rng:
            r = (tc % 2) * H
            zt = sb["zt"]
            self.dma("sp", zt[:, r:r + H], t["zd"][tc * 128:(tc + 1) * 128, :],
                     reads=[("zd", tc, tc + 1)], writes=[("zt", r, r + H)],
                     skey=("zt", r, r + H))
            self.dma("sp", sb["sq"][:, :], t["x0"][tc * 128:(tc + 1) * 128, :],
                     writes=[("sq", 0, H)], skey=("sq", 0, 1))
            S.emit("dve", lambda v, r=r: v.tensor_add(zt[:, r:r + H],
                                                      zt[:, r:r + H],
                                                      sb["sq"][:, :]),
                   reads=[("zt", r, r + H), ("sq", 0, H)],
                   writes=[("zt", r, r + H)])
            self.dma("sp", t["x1d"][tc * 128:(tc + 1) * 128, :], zt[:, r:r + H],
                     reads=[("zt", r, r + H)], writes=[("x1d", tc, tc + 1)],
                     skey=("zt", r, r + H))
            so = (tc % 2) * 8
            self._ln_stats(zt, r, "zt", so)
            S.emit("dve", lambda v, r=r, so=so: v.tensor_scalar(
                sb["xnbf"][:, r:r + H], zt[:, r:r + H], sb["st"][:, so:so + 1],
                sb["st"][:, so + 3:so + 4], ALU.add, ALU.mult),
                reads=[("zt", r, r + H), ("st", so, so + 8)],
                writes=[("xnbf", r, r + H)])
            S.emit("dve", lambda v, r=r, so=so: v.tensor_scalar(
                zt[:, r:r + H], zt[:, r:r + H], sb["st"][:, so:so + 1],
                sb["st"][:, so + 3:so + 4], ALU.add, ALU.mult),
                reads=[("zt", r, r + H), ("st", so, so + 8)],
                writes=[("zt", r, r + H)])
            for kc in range(KC):
                i = tc * KC + kc
                pb = i % 2
                po = (i // 2 % 4) * 128
                pslice = pt[pb][:, po:po + 128]
                self._transpose(pslice, ptk[pb],
                                sb["xnbf"][:, r + kc * 128: r + (kc + 1) * 128],
                                [("xnbf", r + kc * 128, r + (kc + 1) * 128)])
                d = kc * 128
                S.emit("act", lambda a, ps=pslice, d=d: a.copy(
                    sb["xftS"][:, d:d + 128], ps),
                    reads=[ptk[pb]],
                    writes=[("xftS", d, d + 128)])
                pb2 = kc % 2
                self._transpose(pp[pb2][:, 0:128], (ppk[pb2], 0, 128),
                                zt[:, r + kc * 128: r + (kc + 1) * 128],
                                [("zt", r + kc * 128, r + (kc + 1) * 128)],
                                f32=True)
                S.emit("act", lambda a, pb2=pb2, kc=kc: a.copy(
                    sb["xft32"][:, kc * 128:(kc + 1) * 128], pp[pb2][:, 0:128]),
                    reads=[(ppk[pb2], 0, 128)],
                    writes=[("xft32", kc * 128, (kc + 1) * 128)])
            self.dma("act", t["xfin"][:, tc * 128:(tc + 1) * 128].rearrange(
                "(kc q) c -> q kc c", q=128),
                sb["xftS"][:, 0:KC * 128].rearrange("q (kc c) -> q kc c", c=128),
                reads=[("xftS", 0, KC * 128)], writes=[("xfin", tc, tc + 1)],
                skey=("xftS", 0, 1))
            self._mm_acc(pl[:, 0:16], "pl0",
                         lambda kc: sb["xft32"][:, kc * 128:(kc + 1) * 128],
                         lambda kc: sb["gwt"][:, kc * E:(kc + 1) * E],
                         KC,
                         lambda kc: [("xft32", kc * 128, (kc + 1) * 128), "gwt"])
            S.emit("dve", lambda v: v.tensor_copy(sb["cw"][:, 0:16], pl[:, 0:16]),
                   reads=["pl0"], writes=[("cw", 0, 16)])
            self.dma("act", t["cmbin"][tc * 128:(tc + 1) * 128, :], sb["cw"][:, 0:16],
                     reads=[("cw", 0, 16)], writes=[("cmbin", tc, tc + 1)],
                     skey=("cw", 0, 1))

    def moe_exchange(self):
        S, t = self.S, self.t
        rg = [list(range(NC))]
        S.emit("pool", lambda g: g.collective_compute(
            "AllGather", ALU.bypass, rg, [t["xfin"].opt()], [t["xfout"].opt()]),
            reads=[("xfin", 0, TC)], writes=["xfout"], cc=True)
        S.emit("pool", lambda g: g.collective_compute(
            "AllGather", ALU.bypass, rg, [t["cmbin"].opt()], [t["cmbout"].opt()]),
            reads=["cmbin"], writes=["cmbout"], cc=True)

    def combine_tb(self, tb):
        S, sb, t = self.S, self.sb, self.t
        gr = tb * 512
        self.dma("sp", sb["cw"][:, :].rearrange("q (tcl e) -> q tcl e", e=E),
                 t["cmbout"][gr:gr + 512, :].rearrange("(tcl q) e -> q tcl e", q=128),
                 reads=["cmbout"], writes=[("cw", 0, 64)], skey=("cw", 0, 1))
        for tcl in range(4):
            self._softmax_top4(co=tcl * 16)
            for e in range(EPER):
                col = (tb * 4 + tcl) * EPER + e
                S.emit("dve", lambda v, e=e: v.tensor_mul(
                    sb["mk"][:, :], sb["cmb"][:, :],
                    sb["esel"][:, e * E:(e + 1) * E]),
                    reads=["cmb", "esel"], writes=["mk"])
                S.emit("dve", lambda v, col=col: v.reduce_sum(
                    sb["cbc"][:, col:col + 1], sb["mk"][:, :], axis=AX),
                    reads=["mk"], writes=[("cbc", col, col + 1)])

    def _softmax_top4(self, co=0):
        S, sb = self.S, self.sb
        st = sb["st"]
        cwk = ("cw", co, co + 16)
        cws = sb["cw"][:, co:co + 16]
        S.emit("dve", lambda v: v.reduce_max(st[:, 48:49], cws, axis=AX,
                                             negate=True),
               reads=[cwk], writes=[("st", 48, 56)])
        S.emit("act", lambda a: a.activation(cws, cws, AF.Exp,
                                             bias=st[:, 48:49]),
               reads=[cwk, ("st", 48, 56)], writes=[cwk])
        S.emit("dve", lambda v: v.reduce_sum(st[:, 49:50], cws, axis=AX),
               reads=[cwk], writes=[("st", 48, 56)])
        S.emit("dve", lambda v: v.reciprocal(st[:, 50:51], st[:, 49:50]),
               reads=[("st", 48, 56)], writes=[("st", 48, 56)])
        S.emit("dve", lambda v: v.tensor_scalar(cws, cws,
                                                st[:, 50:51], None, ALU.mult),
               reads=[cwk, ("st", 48, 56)], writes=[cwk])
        S.emit("dve", lambda v: v.max(sb["m8"][:, :], cws),
               reads=[cwk], writes=["m8"])
        S.emit("dve", lambda v: v.tensor_scalar(sb["mk"][:, :], cws,
                                                sb["m8"][:, 3:4], None, ALU.is_ge),
               reads=[cwk, "m8"], writes=["mk"])
        S.emit("dve", lambda v: v.tensor_mul(sb["cmb"][:, :], sb["mk"][:, :],
                                             cws),
               reads=["mk", cwk], writes=["cmb"])
        S.emit("dve", lambda v: v.reduce_sum(st[:, 51:52], sb["cmb"][:, :], axis=AX),
               reads=["cmb"], writes=[("st", 48, 56)])
        S.emit("dve", lambda v: v.reciprocal(st[:, 52:53], st[:, 51:52]),
               reads=[("st", 48, 56)], writes=[("st", 48, 56)])
        S.emit("dve", lambda v: v.tensor_scalar(sb["cmb"][:, :], sb["cmb"][:, :],
                                                st[:, 52:53], None, ALU.mult),
               reads=["cmb", ("st", 48, 56)], writes=["cmb"])

    def dma_acc(self, eng, out, in_, accum, reads=(), writes=(), skey=None):
        op = ALU.add if accum else ALU.bypass
        self.S.emit(eng, lambda e, o=out, i=in_, op=op: e.dma_start(
            out=o, in_=i, accum_op=op), reads=reads, writes=writes, dma=True,
            skey=skey)

    def _dequant_tile(self, eng, wkey, wdst, width, scale_reads):
        S, sb = self.S, self.sb
        dq1i = sb["dq1"]
        dq1f = dq1i.bitcast(F32)
        dq2, dq3, dqs = sb["dq2"], sb["dq3"], sb["dqs"]
        w = width
        ks = [("dqw", 0, w), ("dq1", 0, w), ("dq2", 0, w), ("dq3", 0, w)]

        def em(fn, rd, wr):
            S.emit(eng, fn, reads=rd, writes=wr)

        em(lambda v: v.tensor_scalar(dq1i[:, 0:w], sb["dqw"][:, 0:w],
                                     sb["m15c"][:, :], None, ALU.bitwise_and),
           [("dqw", 0, w), "m15c"], [("dq1", 0, w)])
        em(lambda v: v.tensor_copy(dq2[:, 0:w], dq1i[:, 0:w]),
           [("dq1", 0, w)], [("dq2", 0, w)])
        em(lambda v: v.tensor_scalar(dq2[:, 0:w], dq2[:, 0:w], sb["p2nc"][:, :],
                                     None, ALU.mult),
           [("dq2", 0, w), "p2nc"], [("dq2", 0, w)])
        em(lambda v: v.tensor_scalar(dq3[:, 0:w], dq2[:, 0:w], 8.0, 8.0,
                                     ALU.is_ge, ALU.mult),
           [("dq2", 0, w)], [("dq3", 0, w)])
        em(lambda v: v.tensor_sub(dq2[:, 0:w], dq2[:, 0:w], dq3[:, 0:w]),
           [("dq2", 0, w), ("dq3", 0, w)], [("dq2", 0, w)])
        em(lambda v: v.tensor_scalar(dq3[:, 0:w], dq3[:, 0:w], -0.25, 1.0,
                                     ALU.mult, ALU.add),
           [("dq3", 0, w)], [("dq3", 0, w)])
        em(lambda v: v.tensor_copy(dq1i[:, 0:w], dq2[:, 0:w]),
           [("dq2", 0, w)], [("dq1", 0, w)])
        em(lambda v: v.tensor_scalar(dq1i[:, 0:w], dq1i[:, 0:w], 22, None,
                                     ALU.logical_shift_left),
           [("dq1", 0, w)], [("dq1", 0, w)])
        em(lambda v: v.tensor_scalar(dq1i[:, 0:w], dq1i[:, 0:w], 0x3F000000, None,
                                     ALU.add),
           [("dq1", 0, w)], [("dq1", 0, w)])
        em(lambda v: v.tensor_scalar(dq2[:, 0:w], dq1f[:, 0:w], -1.0, 0.0,
                                     ALU.add, ALU.min),
           [("dq1", 0, w)], [("dq2", 0, w)])
        em(lambda v: v.tensor_add(dq2[:, 0:w], dq1f[:, 0:w], dq2[:, 0:w]),
           [("dq1", 0, w), ("dq2", 0, w)], [("dq2", 0, w)])
        em(lambda v: v.tensor_mul(dq2[:, 0:w], dq2[:, 0:w], dq3[:, 0:w]),
           [("dq2", 0, w), ("dq3", 0, w)], [("dq2", 0, w)])
        em(lambda v, wdst=wdst: v.tensor_mul(wdst, dq2[:, 0:w], dqs[:, 0:w]),
           [("dq2", 0, w), ("dqs", 0, w)] + list(scale_reads), [wkey])

    def _vexp_list(self):
        lst = [dict(kind="exp", e=e, q=q) for e in range(EPER) for q in range(4)]
        lst.append(dict(kind="sh"))
        return lst

    def dequant_thunks(self, ve, buf):
        thunks = []
        t = self.t
        wgu, wd = "wgu" + buf, "wd" + buf
        if ve["kind"] == "exp":
            e, q = ve["e"], ve["q"]
            for kc in range(KC):
                def th(kc=kc, e=e, q=q):
                    g0 = q * 384
                    u0 = IFF + q * 384
                    rows = slice(kc * 128, (kc + 1) * 128)
                    self.dma("sp", self.sb["dqw"][:, 0:384],
                             t["gupw"][e][rows, g0:g0 + 384],
                             writes=[("dqw", 0, 384)], skey=("dqw", 0, 384))
                    self.dma("sp", self.sb["dqw"][:, 384:768],
                             t["gupw"][e][rows, u0:u0 + 384],
                             writes=[("dqw", 384, 768)], skey=("dqw", 384, 768))
                    self.dma("sp", self.sb["dqs"][:, 0:384],
                             t["gusr"][e][rows, g0:g0 + 384],
                             writes=[("dqs", 0, 384)], skey=("dqs", 0, 384))
                    self.dma("sp", self.sb["dqs"][:, 384:768],
                             t["gusr"][e][rows, u0:u0 + 384],
                             writes=[("dqs", 384, 768)], skey=("dqs", 384, 768))
                    self._dequant_tile("dve", (wgu, kc * 768, (kc + 1) * 768),
                                       self.sb[wgu][:, kc * 768:(kc + 1) * 768],
                                       768, [])
                thunks.append(th)
            for ic in range(3):
                def th(ic=ic, e=e, q=q):
                    r0 = q * 384 + ic * 128
                    self.dma("sp", self.sb["dqw"][:, 0:2048],
                             t["dnpw"][e][r0:r0 + 128, :],
                             writes=[("dqw", 0, 2048)], skey=("dqw", 0, 384))
                    self.dma("sp", self.sb["dqs"][:, 0:2048],
                             t["dnsr"][e][r0:r0 + 128, :],
                             writes=[("dqs", 0, 2048)], skey=("dqs", 0, 384))
                    self._dequant_tile("dve", (wd, ic * H, (ic + 1) * H),
                                       self.sb[wd][:, ic * H:(ic + 1) * H], 2048, [])
                thunks.append(th)
        else:
            for kc in range(KC):
                def th(kc=kc):
                    rows = slice(kc * 128, (kc + 1) * 128)
                    self.dma("sp", self.sb["dqw"][:, 0:384], t["sgupw"][rows, :],
                             writes=[("dqw", 0, 384)], skey=("dqw", 0, 384))
                    self.dma("sp", self.sb["dqs"][:, 0:384], t["sgusr"][rows, :],
                             writes=[("dqs", 0, 384)], skey=("dqs", 0, 384))
                    self._dequant_tile("dve", (wgu, kc * 768, kc * 768 + 384),
                                       self.sb[wgu][:, kc * 768:kc * 768 + 384],
                                       384, [])
                thunks.append(th)
            for ic, (r0, rn) in enumerate(((0, 128), (128, 64))):
                def th(ic=ic, r0=r0, rn=rn):
                    self.dma("sp", self.sb["dqw"][0:rn, 0:2048],
                             t["sdnpw"][r0:r0 + rn, :],
                             writes=[("dqw", 0, 2048)], skey=("dqw", 0, 384))
                    self.dma("sp", self.sb["dqs"][0:rn, 0:2048],
                             t["sdnsr"][r0:r0 + rn, :],
                             writes=[("dqs", 0, 2048)], skey=("dqs", 0, 384))
                    self._dequant_tile("dve", (wd, ic * H, (ic + 1) * H),
                                       self.sb[wd][:, ic * H:(ic + 1) * H], 2048, [])
                thunks.append(th)
        return thunks

    def ve_matmuls(self, ve, buf, first, pending=()):
        pending = list(pending)
        S, sb, t = self.S, self.sb, self.t
        wgu, wd = sb["wgu" + buf], sb["wd" + buf]
        wguk, wdk = "wgu" + buf, "wd" + buf
        pp = [self.ps["pp0"], self.ps["pp1"]]
        ppk = ["pp0", "pp1"]
        py = [self.ps["py0"], self.ps["py1"]]
        pyk = ["py0", "py1"]
        sh = ve["kind"] == "sh"
        if sh:
            pchunks = [(0, 128), (128, 64)]
            uoff = 192
        else:
            pchunks = [(0, 128), (128, 128), (256, 128)]
            uoff = 384
        for tb in range(32):
            if first:
                if tb == 0:
                    self.combine_tb(0)
                    self.combine_tb(1)
                elif tb + 1 < 32:
                    self.combine_tb(tb + 1)
            if tb >= 4 and pending:
                n_each = max(1, (len(pending) + 27) // 28)
                for _ in range(n_each):
                    if pending:
                        pending.pop(0)()
            r0, c0 = tb // 4, (tb % 4) * CH
            xfn = "xft" if tb % 2 == 0 else "xft2"
            xft = sb[xfn]
            self.dma("sp", xft[:, :].rearrange("q (kc t) -> q kc t", t=CH),
                     t["xfout"][r0 * H:(r0 + 1) * H, c0:c0 + CH].rearrange(
                         "(kc q) t -> q kc t", q=128),
                     reads=["xfout"], writes=[(xfn, 0, KC * CH)],
                     skey=(xfn, 0, 1))
            for pi, (pc0, pw) in enumerate(pchunks):
                bp = [pp, py][pi % 2]
                bpk = [ppk, pyk][pi % 2]
                for gu in range(2):
                    off = pc0 + gu * uoff
                    pb = gu
                    self._mm_acc(bp[pb][0:pw, 0:CH], bpk[pb],
                                 lambda kc, off=off, pw=pw: wgu[:, kc * 768 + off:
                                                                kc * 768 + off + pw],
                                 lambda kc, xft=xft: xft[:, kc * CH:(kc + 1) * CH],
                                 KC,
                                 lambda kc, off=off, pw=pw, xfn=xfn: [
                                     (wguk, kc * 768 + off, kc * 768 + off + pw),
                                     (xfn, kc * CH, (kc + 1) * CH)])
                eo2 = (pi % 2) * CH
                S.emit("act", lambda a, bp=bp, eo2=eo2: a.activation(
                    sb["ev"][:, eo2:eo2 + CH], bp[0][:, 0:CH], AF.Silu),
                    reads=[bpk[0]], writes=[("ev", eo2, eo2 + CH)])
                S.emit("dve", lambda v, pi=pi, bp=bp, eo2=eo2: v.tensor_mul(
                    sb["act"][:, pi * CH:(pi + 1) * CH], sb["ev"][:, eo2:eo2 + CH],
                    bp[1][:, 0:CH]),
                    reads=[("ev", eo2, eo2 + CH), bpk[1]],
                    writes=[("act", pi * CH, (pi + 1) * CH)])
            for tcl in range(4):
                for n2 in range(4):
                    pb = (tcl * 4 + n2) % 2
                    self._mm_acc(
                        py[pb][:, 0:CH], pyk[pb],
                        lambda ic, tcl=tcl: sb["act"][0:pchunks[ic][1],
                                                      ic * CH + tcl * 128:
                                                      ic * CH + tcl * 128 + 128],
                        lambda ic, n2=n2: wd[0:pchunks[ic][1],
                                             ic * H + n2 * CH: ic * H + (n2 + 1) * CH],
                        len(pchunks),
                        lambda ic, tcl=tcl, n2=n2: [
                            ("act", ic * CH + tcl * 128, ic * CH + tcl * 128 + 128),
                            (wdk, ic * H + n2 * CH, ic * H + (n2 + 1) * CH)])
                    yo = tcl * H + n2 * CH
                    if sh:
                        S.emit("dve", lambda v, pb=pb, yo=yo: v.tensor_copy(
                            sb["yblk"][:, yo:yo + CH], py[pb][:, 0:CH]),
                            reads=[pyk[pb]], writes=[("yblk", yo, yo + CH)])
                    else:
                        col = (tb * 4 + tcl) * EPER + ve["e"]
                        S.emit("dve", lambda v, pb=pb, yo=yo, col=col: v.tensor_scalar(
                            sb["yblk"][:, yo:yo + CH], py[pb][:, 0:CH],
                            sb["cbc"][:, col:col + 1], None, ALU.mult),
                            reads=[pyk[pb], ("cbc", col, col + 1)],
                            writes=[("yblk", yo, yo + CH)])
            self.dma_acc("pool",
                         t["part"][tb * CH:(tb + 1) * CH, :].rearrange(
                             "(tcl q) n -> q tcl n", q=128),
                         sb["yblk"][:, :].rearrange("q (tcl n) -> q tcl n", n=H),
                         accum=not first,
                         reads=[("yblk", 0, 4 * H)],
                         writes=[("part", tb, tb + 1)],
                         skey=("yblk", 0, 1))

    def moe_compute(self, first_dequanted=False):
        ves = self._vexp_list()
        if not first_dequanted:
            for th in self.dequant_thunks(ves[0], "A"):
                th()
        for vi, ve in enumerate(ves):
            buf = "A" if vi % 2 == 0 else "B"
            nxt = (self.dequant_thunks(ves[vi + 1], "B" if buf == "A" else "A")
                   if vi + 1 < len(ves) else [])
            self.ve_matmuls(ve, buf, first=(vi == 0), pending=nxt)

    def final(self):
        S, sb, t = self.S, self.sb, self.t
        rg = [list(range(NC))]
        S.emit("pool", lambda g: g.collective_compute(
            "ReduceScatter", ALU.add, rg, [t["part"].opt()], [t["rsout"].opt()]),
            reads=[("part", 0, 32)], writes=["rsout"], cc=True)
        for tc in range(TC // 128):
            r = (tc % 2) * H
            self.dma("sp", sb["zt"][:, r:r + H], t["x1d"][tc * 128:(tc + 1) * 128, :],
                     reads=[("x1d", tc, tc + 1)], writes=[("zt", r, r + H)],
                     skey=("zt", r, r + H))
            self.dma("sp", sb["xnbf"][:, r:r + H], t["rsout"][tc * 128:(tc + 1) * 128, :],
                     reads=["rsout"], writes=[("xnbf", r, r + H)],
                     skey=("xnbf", r, r + H))
            S.emit("dve", lambda v, r=r: v.tensor_add(
                sb["zt"][:, r:r + H], sb["zt"][:, r:r + H], sb["xnbf"][:, r:r + H]),
                reads=[("zt", r, r + H), ("xnbf", r, r + H)],
                writes=[("zt", r, r + H)])
            self.dma("sp", t["yout"][tc * 128:(tc + 1) * 128, :], sb["zt"][:, r:r + H],
                     reads=[("zt", r, r + H)], writes=[("yout", tc, tc + 1)],
                     skey=("zt", r, r + H))
        self.S.wait_only("sp", reads=[("yout", 0, TC // 128)])

    def build(self):
        nc = self.nc
        self.make_arena()
        self.load_consts()
        self.precast()
        self.alloc_attn()
        self.attn_layer(0)
        self.tap2("zl0", self.t["zd"], "zd")
        self.alloc_moe()
        self.attn_layer(1, post_pass=lambda p: self.x1_xf_stage(p))
        self.tap2("zl1", self.t["zd"], "zd")
        self.tap2("x1", self.t["x1d"], "x1d")
        self.moe_exchange()
        ves = self._vexp_list()
        for th in self.dequant_thunks(ves[0], "A"):
            th()
        self.moe_compute(first_dequanted=True)
        self.final()
        S = self.S
        per = S.prepare()
        trks = list(S.cnt.keys())
        assert len(trks) <= 100, f"too many semaphores: {len(trks)}"
        with contextlib.ExitStack() as es:
            sems = {t_: es.enter_context(nc.semaphore("sem%d" % i_))
                    for i_, t_ in enumerate(trks)}
            blk = es.enter_context(nc.Block())

            @blk.tensor
            def _(eng):
                S.replay_engine("pe", eng, sems, per)

            @blk.scalar
            def _(eng):
                S.replay_engine("act", eng, sems, per)

            @blk.vector
            def _(eng):
                S.replay_engine("dve", eng, sems, per)

            @blk.gpsimd
            def _(eng):
                S.replay_engine("pool", eng, sems, per)

            @blk.sync
            def _(eng):
                S.replay_engine("sp", eng, sems, per)
        return nc

    def tap2(self, name, src, skey):
        if name not in self.taps:
            return
        dst = self.t["tap_" + name]
        buf = self.sb["zt"]
        for tc in range(TC // 128):
            r = (tc % 2) * H
            self.dma("sp", buf[:, r:r + H], src[tc * 128:(tc + 1) * 128, :],
                     reads=[(skey, tc, tc + 1)], writes=[("zt", r, r + H)],
                     skey=("zt", r, r + H))
            self.dma("sp", dst[tc * 128:(tc + 1) * 128, :], buf[:, r:r + H],
                     reads=[("zt", r, r + H)], writes=[("tap" + name, tc, tc + 1)],
                     skey=("zt", r, r + H))
        self.S.wait_only("sp", reads=[("tap" + name, 0, TC // 128)])


def kernel(**inputs):
    maps = prep_inputs(inputs)
    kb = KB()
    kb.build()
    res = run_bass_kernel_spmd(kb.nc, maps, core_ids=list(range(NC)))
    out = np.concatenate([res.results[c]["yout"] for c in range(NC)], 0)
    return out.reshape(B, S, H)


# version 12 (final)
# speedup vs baseline: 1.1119x; 1.1119x over previous
import contextlib
import numpy as np
import ml_dtypes
import concourse.bass as bass
import concourse.mybir as mybir
from concourse.bass_utils import run_bass_kernel_spmd

dt = mybir.dt
ALU = mybir.AluOpType
AF = mybir.ActivationFunctionType
AX = mybir.AxisListType.X
BF16 = ml_dtypes.bfloat16
F32 = dt.float32
BF = dt.bfloat16
I32 = dt.int32

B, S, H = 2, 8192, 2048
HEADS, HD = 8, 256
FF = 8192
L = 2
NC = 8
TC = 2048
CH = 512
NP = TC // CH
KC = H // 128
FC = FF // 128
E, TOPK, IFF, GSZ = 16, 4, 1536, 128
EPS = 1e-5
EPER = E // NC
SHC = IFF // NC

GAMMA = 1.0 - np.exp(np.linspace(np.log(1.0 / 32), np.log(1.0 / 512), HEADS))



WHOLE = (0, 1 << 60)


def _norm_raw(keys):
    out = []
    for k in keys:
        if isinstance(k, tuple):
            out.append(k)
        else:
            out.append((k, 0, 1 << 60))
    return out


class Sched:
    EXES = ("pe", "act", "dve", "pool", "sp")

    def __init__(self):
        self.items = []
        self.cnt = {e: 0 for e in self.EXES}
        self.wr = {}
        self.rd = {}
        self.hw = {}
        self.pending = []
        self.vinfo = {}
        self.ncc = 0

    def _norm(self, keys):
        out = []
        for name, lo, hi in _norm_raw(keys):
            vi = self.vinfo.get(name)
            if vi is None:
                out.append((name, lo, hi))
            else:
                off, cols, es = vi
                out.append(("AR", off + lo * es, off + min(hi, cols) * es))
        return out

    def _dep(self, waits, trk, cnt):
        if cnt <= 0:
            return
        waits[trk] = max(waits.get(trk, 0), cnt)

    def emit(self, exe, fn, reads=(), writes=(), dma=False, inc=1, cc=False,
             skey=None):
        assert exe in self.EXES
        reads = self._norm(reads)
        writes = self._norm(writes)
        if dma:
            assert skey is not None
            k = self._norm([skey])[0]
            trk, amt = ("D",) + k, 16
        elif cc:
            trk, amt = ("C", self.ncc), 1
            self.ncc += 1
        else:
            trk, amt = exe, inc
        if trk not in self.cnt:
            self.cnt[trk] = 0
        waits = {}
        if dma and self.cnt[trk]:
            waits[trk] = self.cnt[trk]
        for name, lo, hi in reads:
            for rec in self.wr.get(name, ()):
                if rec[0] < hi and lo < rec[1]:
                    self._dep(waits, rec[2], rec[3])
        for name, lo, hi in writes:
            for rec in self.wr.get(name, ()):
                if rec[0] < hi and lo < rec[1]:
                    self._dep(waits, rec[2], rec[3])
            for rec in self.rd.get(name, ()):
                if rec[0] < hi and lo < rec[1]:
                    self._dep(waits, rec[2], rec[3])
        item = dict(exe=exe, fn=fn, waits=waits, trk=trk, amt=(amt if inc else 0),
                    reads=reads, writes=writes)
        self.items.append(item)
        if inc:
            self.cnt[trk] += amt
            cdone = self.cnt[trk]
            for it2 in self.pending:
                self._record(it2, trk, cdone)
            self.pending.clear()
            self._record(item, trk, cdone)
        else:
            assert trk == "pe", "inc=0 only supported on pe"
            self.pending.append(item)
        return item

    def _record(self, item, trk, cdone):
        for name, lo, hi in item["writes"]:
            lst = self.wr.setdefault(name, [])
            lst[:] = [r for r in lst if not (lo <= r[0] and r[1] <= hi)]
            lst.append([lo, hi, trk, cdone])
        for name, lo, hi in item["reads"]:
            lst = self.rd.setdefault(name, [])
            lst[:] = [r for r in lst if not (lo <= r[0] and r[1] <= hi
                                             and r[2] == trk)]
            lst.append([lo, hi, trk, cdone])

    def wait_only(self, exe, reads=()):
        self.emit(exe, None, reads=reads, inc=1)

    def barrier(self):
        assert not any(self.pending.values())
        snap = dict(self.cnt)
        self.items.append(dict(exe=None, barrier=snap))

    def prepare(self):
        floor = {e: {} for e in self.EXES}
        per = {e: [] for e in self.EXES}
        for it in self.items:
            if it["exe"] is None:
                for e in self.EXES:
                    for t, c in it["barrier"].items():
                        if c > 0:
                            floor[e][t] = max(floor[e].get(t, 0), c)
                continue
            e = it["exe"]
            w = dict(it["waits"])
            for t, c in floor[e].items():
                w[t] = max(w.get(t, 0), c)
            floor[e] = {}
            per[e].append((w, it))
        return per

    def replay_engine(self, e, eng, sems, per):
        hw = self.hw.setdefault(e, {})
        for w, it in per[e]:
            for t, c in sorted(w.items(), key=str):
                if t == e and t == "pe":
                    continue
                if c > hw.get(t, 0):
                    eng.wait_ge(sems[t], c)
                    hw[t] = c
            if it["fn"] is not None:
                inst = it["fn"](eng)
            elif it["amt"]:
                inst = eng.nop()
            else:
                continue
            if it["amt"]:
                inst.then_inc(sems[it["trk"]],
                              it["amt"] if it["amt"] > 1 else 1)



def _tile_w(w):
    K, N = w.shape
    kc, nch = K // 128, N // 128
    t = w.reshape(kc, 128, nch, 128).transpose(2, 1, 0, 3)
    return np.ascontiguousarray(t.reshape(nch, 128, kc * 128))


def _colmaj(v, nch):
    return np.ascontiguousarray(v.reshape(nch, 128).T)


def prep_inputs(inputs):
    x = np.asarray(inputs["L0_attn_x"])
    maps = []

    shared = {}
    for nm, key in (("wq", "L0_attn_Wq"), ("wk", "L0_attn_Wk"),
                    ("wv", "L0_attn_Wv"), ("wg", "L0_attn_Wg"),
                    ("wo", "L0_attn_Wo")):
        w = np.asarray(inputs[key])
        shared[nm] = np.stack([_tile_w(w[l]) for l in range(L)])
    w1 = np.asarray(inputs["L0_attn_W1"])
    shared["w1"] = np.stack([_tile_w(w1[l]) for l in range(L)])
    w2 = np.asarray(inputs["L0_attn_W2"])
    shared["w2"] = np.stack([_tile_w(w2[l]) for l in range(L)])
    shared["lng"] = _colmaj(np.asarray(inputs["L0_attn_ln_gamma"]), KC)
    shared["lnb"] = _colmaj(np.asarray(inputs["L0_attn_ln_beta"]), KC)
    b1 = np.asarray(inputs["L0_attn_b1"])
    shared["b1c"] = np.concatenate([_colmaj(b1[l], FC) for l in range(L)], 1)
    b2 = np.asarray(inputs["L0_attn_b2"])
    shared["b2c"] = np.concatenate([_colmaj(b2[l], KC) for l in range(L)], 1)
    gg = np.asarray(inputs["L0_attn_gn_gamma"])
    gb = np.asarray(inputs["L0_attn_gn_beta"])
    shared["gng"] = np.ascontiguousarray(
        np.broadcast_to(gg[:, None, :], (L, 128, H)))
    shared["gnb"] = np.ascontiguousarray(
        np.broadcast_to(gb[:, None, :], (L, 128, H)))
    gw = np.asarray(inputs["L0_ffn_gate_w"])
    shared["gwt"] = np.ascontiguousarray(
        gw.T.reshape(KC, 128, E).transpose(1, 0, 2).reshape(128, KC * E))

    a = np.arange(CH, dtype=np.float64)
    dtm = np.zeros((HEADS, NP, 128, CH), np.float64)
    for h in range(HEADS):
        for bc in range(NP):
            bpos = bc * 128 + np.arange(128, dtype=np.float64)
            diff = a[None, :] - bpos[:, None]
            dtm[h, bc] = np.where(diff >= 0, GAMMA[h] ** np.maximum(diff, 0), 0.0)
    shared["dtm"] = dtm.astype(BF16)
    gq = np.stack([GAMMA[h] ** (a + 1.0) for h in range(HEADS)])
    shared["gqt"] = np.ascontiguousarray(
        np.broadcast_to(gq.reshape(1, HEADS * CH), (128, HEADS * CH))).astype(BF16)
    kr = np.zeros((128, HEADS * NP), np.float64)
    for h in range(HEADS):
        for mc in range(NP):
            kr[:, h * NP + mc] = GAMMA[h] ** (CH - 1 - (mc * 128 + np.arange(128)))
    shared["krev"] = kr.astype(np.float32)
    ident = np.eye(128, dtype=np.float32)
    shared["ident"] = ident.astype(BF16)
    shared["identf"] = ident
    cst = np.zeros((128, 4), np.float32)
    cst[:, 0] = EPS
    shared["cstf"] = cst
    pm8 = np.arange(128) % 8
    shared["m15c"] = (np.int32(15) << (4 * pm8).astype(np.int32)).reshape(128, 1).astype(np.int32)
    shared["p2nc"] = (2.0 ** (-4.0 * pm8)).reshape(128, 1).astype(np.float32)

    gup = np.asarray(inputs["L0_ffn_gate_up_packed"])
    gus = np.asarray(inputs["L0_ffn_gate_up_scales"])
    dnp = np.asarray(inputs["L0_ffn_down_packed"])
    dns = np.asarray(inputs["L0_ffn_down_scales"])
    sgup = np.asarray(inputs["L0_ffn_shared_gate_up_packed"])
    sgus = np.asarray(inputs["L0_ffn_shared_gate_up_scales"])
    sdnp = np.asarray(inputs["L0_ffn_shared_down_packed"])
    sdns = np.asarray(inputs["L0_ffn_shared_down_scales"])

    def rep_words(p):
        return np.repeat(p, 8, axis=0)

    def rep_scales(s):
        return np.repeat(s, GSZ, axis=0)

    for c in range(NC):
        m = dict(shared)
        b, s0 = c // 4, (c % 4) * TC
        m["x0"] = np.ascontiguousarray(x[b, s0:s0 + TC])

        wcc = np.zeros((128, NC * KC), np.float64)
        for cp in range(NC):
            if cp // 4 == b and cp < c:
                for ch_i in range(KC):
                    h = ch_i // 2
                    wcc[:, cp * KC + ch_i] = GAMMA[h] ** (TC * (c - cp - 1))
        m["wcc"] = wcc.astype(np.float32)

        es = np.zeros((128, EPER * E), np.float32)
        for e in range(EPER):
            es[:, e * E + c * EPER + e] = 1.0
        m["esel"] = es
        e0 = c * EPER
        m["gupw"] = np.ascontiguousarray(
            np.stack([rep_words(gup[e0 + i]) for i in range(EPER)]))
        m["gusr"] = np.ascontiguousarray(
            np.stack([rep_scales(gus[e0 + i]) for i in range(EPER)]))
        m["dnpw"] = np.ascontiguousarray(
            np.stack([rep_words(dnp[e0 + i]) for i in range(EPER)]))
        m["dnsr"] = np.ascontiguousarray(
            np.stack([rep_scales(dns[e0 + i]) for i in range(EPER)]))
        j0 = c * SHC
        cols = np.concatenate([np.arange(j0, j0 + SHC),
                               np.arange(IFF + j0, IFF + j0 + SHC)])
        m["sgupw"] = np.ascontiguousarray(rep_words(sgup)[:, cols])
        m["sgusr"] = np.ascontiguousarray(rep_scales(sgus)[:, cols])
        m["sdnpw"] = np.ascontiguousarray(rep_words(sdnp[j0 // 8:(j0 + SHC) // 8]))
        m["sdnsr"] = np.ascontiguousarray(rep_scales(sdns)[j0:j0 + SHC])
        maps.append(m)
    return maps


INPUT_SPECS = {
    "x0": ((TC, H), F32),
    "wq": ((L, KC, 128, H), F32), "wk": ((L, KC, 128, H), F32),
    "wv": ((L, KC, 128, H), F32), "wg": ((L, KC, 128, H), F32),
    "wo": ((L, KC, 128, H), F32),
    "w1": ((L, FC, 128, H), F32), "w2": ((L, KC, 128, FF), F32),
    "lng": ((128, KC), F32), "lnb": ((128, KC), F32),
    "b1c": ((128, L * FC), F32), "b2c": ((128, L * KC), F32),
    "gng": ((L, 128, H), F32), "gnb": ((L, 128, H), F32),
    "gwt": ((128, KC * E), F32),
    "dtm": ((HEADS, NP, 128, CH), BF), "gqt": ((128, HEADS * CH), BF),
    "krev": ((128, HEADS * NP), F32), "ident": ((128, 128), BF),
    "m15c": ((128, 1), I32), "p2nc": ((128, 1), F32),
    "esel": ((128, EPER * E), F32), "identf": ((128, 128), F32),
    "cstf": ((128, 4), F32),
    "wcc": ((128, NC * KC), F32),
    "gupw": ((EPER, H, 2 * IFF), I32), "gusr": ((EPER, H, 2 * IFF), F32),
    "dnpw": ((EPER, IFF, H), I32), "dnsr": ((EPER, IFF, H), F32),
    "sgupw": ((H, 2 * SHC), I32), "sgusr": ((H, 2 * SHC), F32),
    "sdnpw": ((SHC, H), I32), "sdnsr": ((SHC, H), F32),
}



class KB:

    def __init__(self, taps=()):
        self.taps = tuple(taps)
        self.nc = bass.Bass("TRN2", target_bir_lowering=False, debug=False)
        self.S = Sched()
        self.t = {}
        self.sb = {}
        self.ps = {}
        self._stack = contextlib.ExitStack()
        self._declare_dram()

    def _declare_dram(self):
        nc, t = self.nc, self.t
        for name, (shape, dty) in INPUT_SPECS.items():
            t[name] = nc.dram_tensor(name, list(shape), dty, kind="ExternalInput").ap()
        t["yout"] = nc.dram_tensor("yout", [TC, H], F32, kind="ExternalOutput").ap()
        for nm, sh in (("wqb", (L, KC, 128, H)), ("wkb", (L, KC, 128, H)),
                       ("wvb", (L, KC, 128, H)), ("wgb", (L, KC, 128, H)),
                       ("wob", (L, KC, 128, H)), ("w1b", (L, FC, 128, H)),
                       ("w2b", (L, KC, 128, FF))):
            t[nm] = nc.dram_tensor(nm, list(sh), BF, kind="Internal").ap()
        t["zd"] = nc.dram_tensor("zd", [TC, H], F32, kind="Internal").ap()
        t["x1d"] = nc.dram_tensor("x1d", [TC, H], F32, kind="Internal").ap()
        t["qsp"] = nc.dram_tensor("qsp", [H, TC], BF, kind="Internal").ap()
        t["gsp"] = nc.dram_tensor("gsp", [H, TC], BF, kind="Internal").ap()
        t["ysp"] = nc.dram_tensor("ysp", [TC, H], BF, kind="Internal").ap()
        t["vin"] = nc.dram_tensor("vin", [KC * 128, HD], F32, kind="Internal").ap()
        t["vout"] = nc.dram_tensor("vout", [NC * KC * 128, HD], F32, kind="Internal").ap()
        t["xfin"] = nc.dram_tensor("xfin", [H, TC], BF, kind="Internal").ap()
        t["xfout"] = nc.dram_tensor("xfout", [NC * H, TC], BF, kind="Internal").ap()
        t["cmbin"] = nc.dram_tensor("cmbin", [TC, E], F32, kind="Internal").ap()
        t["cmbout"] = nc.dram_tensor("cmbout", [B * S, E], F32, kind="Internal").ap()
        t["part"] = nc.dram_tensor("part", [B * S, H], BF, kind="Internal").ap()
        t["rsout"] = nc.dram_tensor("rsout", [TC, H], BF, kind="Internal").ap()
        for tp in self.taps:
            t["tap_" + tp] = nc.dram_tensor("tap_" + tp, [TC, H], F32,
                                            kind="ExternalOutput").ap()

    ARENA_BYTES = 212736

    def make_arena(self):
        self._ar = self._stack.enter_context(
            self.nc.sbuf_tensor("AR", [128, self.ARENA_BYTES // 2], BF))
        self._ar_bf = self._ar.ap()
        self._ar_f32 = self._ar.ap().bitcast(F32)
        self._ar_i32 = self._ar.ap().bitcast(I32)

    def view(self, name, cursor, cols, dty):
        es = 2 if dty == BF else 4
        off = (cursor + es - 1) // es * es
        ar = {BF: self._ar_bf, F32: self._ar_f32, I32: self._ar_i32}[dty]
        self.sb[name] = ar[:, off // es: off // es + cols]
        self.S.vinfo[name] = (off, cols, es)
        end = off + cols * es
        assert end <= self.ARENA_BYTES, f"arena overflow at {name}: {end}"
        return end

    def alloc(self, name, shape, dty, psum=False):
        assert psum
        h = self._stack.enter_context(self.nc.psum_tensor(shape, dty))
        self.ps[name] = h
        return h

    def dma(self, eng, out, in_, reads=(), writes=(), skey=None):
        self.S.emit(eng, lambda e, o=out, i=in_: e.dma_start(out=o, in_=i),
                    reads=reads, writes=writes, dma=True, skey=skey)

    def tap(self, name, src):
        if name not in self.taps:
            return
        dst = self.t["tap_" + name]
        buf = self.sb["zt"]
        for tc in range(TC // 128):
            r = (tc % 2) * H
            self.dma("sp", buf[:, r:r + H], src[tc * 128:(tc + 1) * 128, :],
                     reads=[(name + "_src", tc, tc + 1)],
                     writes=[("zt", r, r + H)])
            self.dma("sp", dst[tc * 128:(tc + 1) * 128, :], buf[:, r:r + H],
                     reads=[("zt", r, r + H)], writes=[("tap", tc, tc + 1)])
        self.S.barrier()

    def precast(self):
        t = self.t
        plan = []
        for l in range(L):
            for j in range(KC):
                plan.append(("wg", "wgb", l, j))
            for j in range(KC):
                plan += [("wq", "wqb", l, j), ("wk", "wkb", l, j),
                         ("wv", "wvb", l, j)]
            for j in range(KC):
                plan.append(("wo", "wob", l, j))
            for j in range(FC):
                plan.append(("w1", "w1b", l, j))
            for j in range(KC):
                plan.append(("w2", "w2b", l, j))
        for i, (src, dst, l, j) in enumerate(plan):
            self.dma("pool", t[dst][l, j], t[src][l, j],
                     writes=[(dst, l * 64 + j, l * 64 + j + 1)],
                     skey=("pc%d" % (i % 4), 0, 1))

    def load_consts(self):
        t = self.t
        c = 0
        c = self.view("ident", c, 128, BF)
        c = self.view("identf", c, 128, F32)
        c = self.view("cstf", c, 4, F32)
        c = self.view("lng", c, KC, F32)
        c = self.view("lnb", c, KC, F32)
        c = self.view("b1c", c, L * FC, F32)
        c = self.view("b2c", c, L * KC, F32)
        c = self.view("gwt", c, KC * E, F32)
        c = self.view("m15c", c, 1, I32)
        c = self.view("p2nc", c, 1, F32)
        c = self.view("esel", c, EPER * E, F32)
        c = self.view("zt", c, 2 * H, F32)
        c = self.view("xnbf", c, 2 * H, BF)
        c = self.view("st", c, 64, F32)
        c = self.view("cw", c, 64, F32)
        c = self.view("m8", c, 8, F32)
        c = self.view("mk", c, 16, F32)
        c = self.view("cmb", c, 16, F32)
        c = self.view("sq", c, H, F32)
        c = self.view("sqg", c, HD, F32)
        c = self.view("ev", c, 2 * CH, BF)
        self._glob_end = c
        self.nc.const_aps.register(F32, float(EPS), self.sb["cstf"][:, 0:1])
        for nm in ("ident", "identf", "lng", "lnb", "b1c", "b2c", "gwt", "m15c",
                   "p2nc", "esel", "cstf"):
            self.dma("sp", self.sb[nm][:, :], t[nm], writes=[nm],
                     skey=("cload", 0, 1))

    def z_init(self):
        t, sb, S = self.t, self.sb, self.S
        for tc in range(TC // 128):
            self._ln_chunk(src=t["x0"][tc * 128:(tc + 1) * 128, :], tc=tc,
                           affine=False, out_dram=t["zd"], out_key="zd")

    def _ln_chunk(self, src, tc, affine, out_dram, out_key, pre_add=None):
        sb, S = self.sb, self.S
        r = (tc % 2) * H
        zt, st = sb["zt"], sb["st"]
        self.dma("sp", zt[:, r:r + H], src, writes=[("zt", r, r + H)],
                 skey=("zt", r, r + H))
        if pre_add is not None:
            r2 = ((tc + 1) % 2) * H
            self.dma("sp", zt[:, r2:r2 + H], pre_add, writes=[("zt", r2, r2 + H)],
                     skey=("zt", r2, r2 + H))
            S.emit("dve", lambda v, a=zt[:, r:r + H], b=zt[:, r2:r2 + H]:
                   v.tensor_add(a, a, b),
                   reads=[("zt", r, r + H), ("zt", r2, r2 + H)],
                   writes=[("zt", r, r + H)])
        self._ln_stats_apply(zt, r, tc, affine=affine)
        self.dma("sp", out_dram[tc * 128:(tc + 1) * 128, :], zt[:, r:r + H],
                 reads=[("zt", r, r + H)], writes=[(out_key, tc, tc + 1)],
                 skey=("zt", r, r + H))

    def _ln_stats_apply(self, zt, r, tag, affine):
        sb, S = self.sb, self.S
        st, sq = sb["st"], sb["sq"]
        so = (tag % 2) * 8
        ztk = ("zt", r, r + H)
        stk = ("st", so, so + 8)
        sqk = ("sq", 0, H)
        S.emit("dve", lambda v: v.reduce_sum(st[:, so:so + 1], zt[:, r:r + H],
                                             axis=AX, negate=True),
               reads=[ztk], writes=[stk])
        S.emit("act", lambda a: a.activation(sq[:], zt[:, r:r + H], AF.Square,
                                             accum_out=st[:, so + 1:so + 2]),
               reads=[ztk], writes=[stk, sqk])
        S.emit("dve", lambda v: v.tensor_scalar(st[:, so:so + 1], st[:, so:so + 1],
                                                1.0 / H, None, ALU.mult),
               reads=[stk], writes=[stk])
        S.emit("dve", lambda v: v.tensor_tensor(st[:, so + 2:so + 3],
                                                st[:, so:so + 1], st[:, so:so + 1],
                                                ALU.mult),
               reads=[stk], writes=[stk])
        S.emit("dve", lambda v: v.scalar_tensor_tensor(st[:, so + 2:so + 3],
                                                       st[:, so + 1:so + 2],
                                                       1.0 / H, st[:, so + 2:so + 3],
                                                       ALU.mult, ALU.subtract),
               reads=[stk], writes=[stk])
        S.emit("act", lambda a: a.activation(st[:, so + 2:so + 3],
                                             st[:, so + 2:so + 3], AF.Sqrt,
                                             bias=float(EPS)),
               reads=[stk, "cstf"], writes=[stk])
        S.emit("dve", lambda v: v.reciprocal(st[:, so + 3:so + 4],
                                             st[:, so + 2:so + 3]),
               reads=[stk], writes=[stk])
        S.emit("dve", lambda v: v.tensor_scalar(zt[:, r:r + H], zt[:, r:r + H],
                                                st[:, so:so + 1], st[:, so + 3:so + 4],
                                                ALU.add, ALU.mult),
               reads=[ztk, stk], writes=[ztk])

    def alloc_attn(self):
        c = self._glob_end
        c = self.view("xnT", c, KC * CH, BF)
        c = self.view("wsl", c, 2 * H, BF)
        c = self.view("qh", c, 2 * CH, BF)
        c = self.view("kh", c, 2 * CH, BF)
        c = self.view("vfm", c, 2 * CH, BF)
        c = self.view("qsh", c, 2 * CH, BF)
        c = self.view("yh", c, NP * HD, BF)
        c = self.view("gh", c, 2 * CH, BF)
        yf_at = c
        c = self.view("vtm", c, NP * HD, BF)
        c = self.view("ktm", c, NP * HD, BF)
        c = self.view("ktms", c, NP * HD, BF)
        c = self.view("sdth", c, NP * CH, BF)
        c = self.view("dts", c, 4 * CH, BF)
        c = self.view("gevb", c, 2 * CH, BF)
        assert c - yf_at == HEADS * NP * HD * 2
        self.view("yfull", yf_at, HEADS * NP * HD, BF)
        c = self.view("sst", c, KC * HD, F32)
        c = self.view("sbf", c, KC * HD, BF)
        c = self.view("vtl", c, 2 * HD, F32)
        h1g_at = c
        c = self.view("h1g", c, 16 * CH, BF)
        self.view("vtlB0", h1g_at, KC * HD, F32)
        self.view("xnT2", h1g_at, KC * CH, BF)
        retaF_at = c
        c = self.view("retaF", c, NP * H, F32)
        self.view("vtlB1", retaF_at, KC * HD, F32)
        cb = retaF_at
        for nm, cols in (("qhB", 2 * CH), ("khB", 2 * CH), ("vfmB", 2 * CH),
                         ("qshB", 2 * CH), ("vtmB", NP * HD), ("ktmB", NP * HD),
                         ("ktmsB", NP * HD), ("sdthB", NP * CH), ("dtsB", 4 * CH)):
            cb = self.view(nm, cb, cols, BF)
        assert cb <= c
        c = self.view("gqt", c, HEADS * CH, BF)
        c = self.view("krev", c, HEADS * NP, F32)
        c = self.view("wcc", c, NC * KC, F32)
        c = self.view("gngl", c, H, F32)
        c = self.view("gnbl", c, H, F32)
        c = self.view("xftS", c, KC * 128, BF)
        c = self.view("xft32", c, KC * 128, F32)
        self._attn_end = c
        for nm in ("gqt", "krev", "wcc"):
            self.dma("sp", self.sb[nm][:, :], self.t[nm], writes=[nm],
                     skey=("cload", 0, 1))
        self.alloc("pp0", [128, 512], F32, psum=True)
        self.alloc("pp1", [128, 512], F32, psum=True)
        self.alloc("py0", [128, 512], F32, psum=True)
        self.alloc("py1", [128, 512], F32, psum=True)
        self.alloc("pt0", [128, 1024], BF, psum=True)
        self.alloc("pt1", [128, 1024], BF, psum=True)
        self.alloc("pl0", [128, 512], F32, psum=True)
        self.alloc("pl1", [128, 512], F32, psum=True)

    def _mm_acc(self, psum, pkey, lhs_fn, rhs_fn, n, reads):
        S = self.S
        for i in range(n):
            lf, rf = lhs_fn(i), rhs_fn(i)
            S.emit("pe",
                   lambda e, a=lf, b=rf, st=(i == 0), sp=(i == n - 1), ps=psum:
                   e.matmul(ps, a, b, start=st, stop=sp),
                   reads=reads(i), writes=[pkey], inc=(1 if i == n - 1 else 0))

    def _transpose(self, out_psum, pkey, in_ap, reads, f32=False):
        ident = self.sb["identf" if f32 else "ident"]
        ikey = "identf" if f32 else "ident"
        self.S.emit("pe", lambda e, o=out_psum, i=in_ap, idt=ident:
                    e.transpose(o, i, idt[:, :]),
                    reads=list(reads) + [ikey], writes=[pkey])

    def _slab(self, which, l, j, rot, cols=None, src_cols=None):
        wsl = self.sb["wsl"]
        w = self.t[which][l, j]
        if src_cols is not None:
            w = w[:, src_cols[0]:src_cols[1]]
        n = cols if cols is not None else w.shape[-1]
        dst = wsl[:, rot * H: rot * H + n]
        self.dma("sp", dst, w, reads=[(which, l * 64 + j, l * 64 + j + 1)],
                 writes=[("wsl", rot * H, rot * H + n)],
                 skey=("wsl", rot * H, rot * H + 1))
        return wsl, rot * H

    def attn_phase1(self, l, p):
        nc, S, sb, t = self.nc, self.S, self.sb, self.t
        xname = "xnT" if p % 2 == 0 else "xnT2"
        xnT, wsl = sb[xname], sb["wsl"]
        pp = [self.ps["pp0"], self.ps["pp1"]]
        py = [self.ps["py0"], self.ps["py1"]]
        pt = [self.ps["pt0"], self.ps["pt1"]]
        ppk = ["pp0", "pp1"]
        pyk = ["py0", "py1"]
        ptk = ["pt0", "pt1"]

        if p > 0:
            for ch in range(KC):
                sl = slice(ch * HD, (ch + 1) * HD)
                S.emit("dve", lambda v, sl=sl: v.tensor_copy(
                    sb["sbf"][:, sl], sb["sst"][:, sl]),
                    reads=[("sst", ch * HD, (ch + 1) * HD)],
                    writes=[("sbf", ch * HD, (ch + 1) * HD)])

        for tc4 in range(4):
            tcg = p * 4 + tc4
            r = (tc4 % 2) * H
            zt, xnbf = sb["zt"], sb["xnbf"]
            if l == 0:
                self.dma("sp", zt[:, r:r + H], t["x0"][tcg * 128:(tcg + 1) * 128, :],
                         writes=[("zt", r, r + H)], skey=("zt", r, r + H))
                self._ln_stats_apply(zt, r, tc4, affine=False)
                self.dma("sp", t["zd"][tcg * 128:(tcg + 1) * 128, :], zt[:, r:r + H],
                         reads=[("zt", r, r + H)], writes=[("zd", tcg, tcg + 1)],
                         skey=("zt", r, r + H))
            else:
                self.dma("sp", zt[:, r:r + H], t["zd"][tcg * 128:(tcg + 1) * 128, :],
                         reads=[("zd", tcg, tcg + 1)], writes=[("zt", r, r + H)],
                         skey=("zt", r, r + H))
            self._ln_stats_apply(zt, r, tc4, affine=False)
            S.emit("dve", lambda v, r=r: v.tensor_copy(xnbf[:, r:r + H],
                                                       zt[:, r:r + H]),
                   reads=[("zt", r, r + H)], writes=[("xnbf", r, r + H)])
            for kc in range(KC):
                pb = (tc4 * KC + kc) % 2
                po = ((tc4 * KC + kc) // 2 % 4) * 128
                pslice = pt[pb][:, po:po + 128]
                self._transpose(pslice, ptk[pb],
                                xnbf[:, r + kc * 128: r + (kc + 1) * 128],
                                [("xnbf", r + kc * 128, r + (kc + 1) * 128)])
                dstc = kc * CH + tc4 * 128
                S.emit("dve", lambda v, ps=pslice, d=dstc, kc=kc: v.tensor_scalar(
                    xnT[:, d:d + 128], ps, sb["lng"][:, kc:kc + 1],
                    sb["lnb"][:, kc:kc + 1], ALU.mult, ALU.add),
                    reads=[ptk[pb], "lng", "lnb"],
                    writes=[(xname, dstc, dstc + 128)])

        for dc in range(KC):
            rot = dc % 2
            wslh, wo = self._slab("wgb", l, dc, rot)
            pb = dc % 2
            self._mm_acc(pp[pb][:, 0:CH], ppk[pb],
                         lambda kc, wo=wo: wsl[:, wo + kc * 128: wo + (kc + 1) * 128],
                         lambda kc: xnT[:, kc * CH:(kc + 1) * CH],
                         KC,
                         lambda kc, wo=wo: [("wsl", wo + kc * 128, wo + (kc + 1) * 128),
                                            (xname, kc * CH, (kc + 1) * CH)])
            gev = sb["gevb"]
            go = rot * CH
            S.emit("act", lambda a, pb=pb, go=go: a.activation(
                gev[:, go:go + CH], pp[pb][:, 0:CH], AF.Silu),
                reads=[ppk[pb]], writes=[("gevb", go, go + CH)])
            self.dma("act", t["gsp"][dc * 128:(dc + 1) * 128, p * CH:(p + 1) * CH],
                     gev[:, go:go + CH],
                     reads=[("gevb", go, go + CH)], writes=[("gsp", dc, dc + 1)],
                     skey=("gevb", go, go + CH))

        def sfx(h):
            return "" if h % 2 == 0 else "B"

        def prep(h):
            sx = sfx(h)
            qh, kh, vfm = sb["qh" + sx], sb["kh" + sx], sb["vfm" + sx]
            vtm, ktm, ktms, qsh = (sb["vtm" + sx], sb["ktm" + sx],
                                   sb["ktms" + sx], sb["qsh" + sx])
            self.dma("sp", sb["dts" + sx][:, :],
                     t["dtm"][h].rearrange("bc q a -> q bc a"),
                     writes=[("dts" + sx, 0, 4 * CH)], skey=("dts" + sx, 0, 1))
            for hf in range(2):
                dc = 2 * h + hf
                for wname, dst, dkey in (("wqb", qh, "qh" + sx),
                                         ("wkb", kh, "kh" + sx),
                                         ("wvb", vfm, "vfm" + sx)):
                    rot = (dc + (0 if wname != "wvb" else 1)) % 2
                    wslh, wo = self._slab(wname, l, dc, rot)
                    pb = hf
                    self._mm_acc(pp[pb][:, 0:CH], ppk[pb],
                                 lambda kc, wo=wo: wsl[:, wo + kc * 128: wo + (kc + 1) * 128],
                                 lambda kc: xnT[:, kc * CH:(kc + 1) * CH],
                                 KC,
                                 lambda kc, wo=wo: [("wsl", wo + kc * 128, wo + (kc + 1) * 128),
                                                    (xname, kc * CH, (kc + 1) * CH)])
                    S.emit("act", lambda a, d=dst, o=hf * CH, pb=pb: a.copy(
                        d[:, o:o + CH], pp[pb][:, 0:CH]),
                        reads=[ppk[pb]], writes=[(dkey, hf * CH, (hf + 1) * CH)])
            for gi, (src, skeyn, dst, dkey) in enumerate(
                    ((vfm, "vfm" + sx, vtm, "vtm" + sx),
                     (kh, "kh" + sx, ktm, "ktm" + sx))):
                pb = (h + gi) % 2
                for hf in range(2):
                    for tcl in range(4):
                        i = hf * 4 + tcl
                        pslice = pt[pb][:, i * 128:(i + 1) * 128]
                        self._transpose(pslice, ptk[pb],
                                        src[:, hf * CH + tcl * 128: hf * CH + (tcl + 1) * 128],
                                        [(skeyn, hf * CH + tcl * 128,
                                          hf * CH + (tcl + 1) * 128)])
                S.emit("act", lambda a, pb=pb, dst=dst: a.copy(
                    dst[:, 0:NP * HD].rearrange("q (tcl hf c) -> q hf tcl c",
                                                hf=2, c=128),
                    pt[pb][:, 0:1024].rearrange("q (hf tcl c) -> q hf tcl c",
                                                tcl=4, c=128)),
                    reads=[ptk[pb]], writes=[(dkey, 0, NP * HD)])
            for mc in range(NP):
                S.emit("act", lambda a, mc=mc, h=h, ktms=ktms, ktm=ktm: a.mul(
                    ktms[:, mc * HD:(mc + 1) * HD], ktm[:, mc * HD:(mc + 1) * HD],
                    sb["krev"][:, h * NP + mc: h * NP + mc + 1]),
                    reads=[("ktm" + sx, mc * HD, (mc + 1) * HD), "krev"],
                    writes=[("ktms" + sx, mc * HD, (mc + 1) * HD)])
            for hf in range(2):
                S.emit("dve", lambda v, hf=hf, h=h, qsh=qsh, qh=qh, sx=sx: v.tensor_mul(
                    qsh[:, hf * CH:(hf + 1) * CH], qh[:, hf * CH:(hf + 1) * CH],
                    sb["gqt"][:, h * CH:(h + 1) * CH]),
                    reads=[("qh" + sx, hf * CH, (hf + 1) * CH), "gqt"],
                    writes=[("qsh" + sx, hf * CH, (hf + 1) * CH)])
            self.dma("act",
                     t["qsp"][2 * h * 128:(2 * h + 2) * 128,
                              p * CH:(p + 1) * CH].rearrange(
                                  "(hf q) a -> q hf a", q=128),
                     qsh[:, :].rearrange("q (hf a) -> q hf a", a=CH),
                     reads=[("qsh" + sx, 0, 2 * CH)],
                     writes=[("qsp", 2 * h, 2 * h + 2)],
                     skey=("qsh" + sx, 0, 1))

        def rets(h):
            sx = sfx(h)
            qh, kh = sb["qh" + sx], sb["kh" + sx]
            vtm, ktms, qsh, sdth = (sb["vtm" + sx], sb["ktms" + sx],
                                    sb["qsh" + sx], sb["sdth" + sx])
            yh = sb["yh"]
            for bc in range(NP):
                pb = bc % 2
                self._mm_acc(pl[pb][:, 0:CH], plk[pb],
                             lambda hf, bc=bc: kh[:, hf * CH + bc * 128: hf * CH + bc * 128 + 128],
                             lambda hf: qh[:, hf * CH:(hf + 1) * CH],
                             2,
                             lambda hf, bc=bc: [("kh" + sx, hf * CH + bc * 128,
                                                 hf * CH + bc * 128 + 128),
                                                ("qh" + sx, hf * CH, (hf + 1) * CH)])
                S.emit("dve", lambda v, pb=pb, bc=bc, sdth=sdth, sx=sx: v.tensor_mul(
                    sdth[:, bc * CH:(bc + 1) * CH], pl[pb][:, 0:CH],
                    sb["dts" + sx][:, bc * CH:(bc + 1) * CH]),
                    reads=[plk[pb], ("dts" + sx, bc * CH, (bc + 1) * CH)],
                    writes=[("sdth" + sx, bc * CH, (bc + 1) * CH)])
            for ac in range(NP):
                pb = ac % 2
                nmm = NP + (2 if p > 0 else 0)

                def lhs(i, ac=ac, sdth=sdth, qsh=qsh):
                    if i < NP:
                        return sdth[:, i * CH + ac * 128: i * CH + ac * 128 + 128]
                    hf = i - NP
                    return qsh[:, hf * CH + ac * 128: hf * CH + ac * 128 + 128]

                def rhs(i, h=h, vtm=vtm):
                    if i < NP:
                        return vtm[:, i * HD:(i + 1) * HD]
                    hf = i - NP
                    return sb["sbf"][:, (2 * h + hf) * HD:(2 * h + hf + 1) * HD]

                def rds(i, ac=ac, h=h, sx=sx):
                    if i < NP:
                        return [("sdth" + sx, i * CH + ac * 128, i * CH + ac * 128 + 128),
                                ("vtm" + sx, i * HD, (i + 1) * HD)]
                    hf = i - NP
                    return [("qsh" + sx, hf * CH + ac * 128, hf * CH + ac * 128 + 128),
                            ("sbf", (2 * h + hf) * HD, (2 * h + hf + 1) * HD)]

                self._mm_acc(py[pb][:, 0:HD], pyk[pb], lhs, rhs, nmm, rds)
                S.emit("dve", lambda v, pb=pb, ac=ac: v.tensor_copy(
                    yh[:, ac * HD:(ac + 1) * HD], py[pb][:, 0:HD]),
                    reads=[pyk[pb]], writes=[("yh", ac * HD, (ac + 1) * HD)])
            self.dma("act",
                     t["ysp"][p * CH:(p + 1) * CH,
                              h * HD:(h + 1) * HD].rearrange(
                                  "(ac q) d -> q ac d", q=128),
                     yh[:, :].rearrange("q (ac d) -> q ac d", d=HD),
                     reads=[("yh", 0, NP * HD)],
                     writes=[("ysp", p * 8 + h, p * 8 + h + 1)],
                     skey=("yh", 0, 1))
            for hf in range(2):
                ch = 2 * h + hf
                pb = hf
                self._mm_acc(pl[pb][:, 0:HD], plk[pb],
                             lambda mc, hf=hf, ktms=ktms: ktms[:, mc * HD + hf * 128: mc * HD + hf * 128 + 128],
                             lambda mc, vtm=vtm: vtm[:, mc * HD:(mc + 1) * HD],
                             NP,
                             lambda mc, hf=hf, sx=sx: [("ktms" + sx, mc * HD + hf * 128,
                                                        mc * HD + hf * 128 + 128),
                                                       ("vtm" + sx, mc * HD, (mc + 1) * HD)])
                g512 = float(GAMMA[h] ** CH)
                S.emit("dve", lambda v, ch=ch, pb=pb, g=g512: v.scalar_tensor_tensor(
                    sb["sst"][:, ch * HD:(ch + 1) * HD],
                    sb["sst"][:, ch * HD:(ch + 1) * HD], g, pl[pb][:, 0:HD],
                    ALU.mult, ALU.add),
                    reads=[("sst", ch * HD, (ch + 1) * HD), plk[pb]],
                    writes=[("sst", ch * HD, (ch + 1) * HD)])

        pl = [self.ps["pl0"], self.ps["pl1"]]
        plk = ["pl0", "pl1"]
        prep(0)
        for h in range(HEADS):
            if h + 1 < HEADS:
                prep(h + 1)
            rets(h)

    def state_exchange(self, l):
        S, sb, t = self.S, self.sb, self.t
        self.dma("sp", t["vin"].rearrange("(ch q) d -> q ch d", q=128),
                 sb["sst"][:].rearrange("q (ch d) -> q ch d", d=HD),
                 reads=[("sst", 0, KC * HD)], writes=["vin"],
                 skey=("sst", 0, 1))
        rg = [list(range(NC))]
        S.emit("pool", lambda g: g.collective_compute(
            "AllGather", ALU.bypass, rg, [t["vin"].opt()], [t["vout"].opt()]),
            reads=["vin"], writes=["vout"], cc=True)
        for cp in range(NC):
            vn = "vtlB%d" % (cp % 2)
            vtb = sb[vn]
            self.dma("sp", vtb[:, :].rearrange("q (ch d) -> q ch d", d=HD),
                     t["vout"][cp * KC * 128:(cp + 1) * KC * 128, :].rearrange(
                         "(ch q) d -> q ch d", q=128),
                     reads=["vout"], writes=[(vn, 0, KC * HD)],
                     skey=(vn, 0, 1))
            for ch in range(KC):
                sl = slice(ch * HD, (ch + 1) * HD)
                wc = cp * KC + ch
                if cp == 0:
                    S.emit("dve", lambda v, sl=sl, wc=wc, vtb=vtb: v.tensor_scalar(
                        sb["sst"][:, sl], vtb[:, sl],
                        sb["wcc"][:, wc:wc + 1], None, ALU.mult),
                        reads=[(vn, sl.start, sl.stop), "wcc"],
                        writes=[("sst", sl.start, sl.stop)])
                else:
                    S.emit("dve", lambda v, sl=sl, wc=wc, vtb=vtb: v.scalar_tensor_tensor(
                        sb["sst"][:, sl], vtb[:, sl],
                        sb["wcc"][:, wc:wc + 1], sb["sst"][:, sl],
                        ALU.mult, ALU.add),
                        reads=[(vn, sl.start, sl.stop), "wcc",
                               ("sst", sl.start, sl.stop)],
                        writes=[("sst", sl.start, sl.stop)])

    def ph2_stage1_thunks(self, l, p):
        S, sb, t = self.S, self.sb, self.t
        pl = [self.ps["pl0"], self.ps["pl1"]]
        plk = ["pl0", "pl1"]
        thunks = []

        def sbf_th():
            for ch in range(KC):
                h = ch // 2
                gp = float(GAMMA[h] ** (CH * p))
                sl = slice(ch * HD, (ch + 1) * HD)
                S.emit("dve", lambda v, sl=sl, g=gp: v.tensor_scalar(
                    sb["sbf"][:, sl], sb["sst"][:, sl], g, None, ALU.mult),
                    reads=[("sst", sl.start, sl.stop)],
                    writes=[("sbf", sl.start, sl.stop)])
        thunks.append(sbf_th)

        def head_th(h):
            qsh, yh = sb["qsh"], sb["yh"]
            yfull = sb["yfull"]
            self.dma("sp", qsh[:, :].rearrange("q (hf a) -> q hf a", a=CH),
                     t["qsp"][2 * h * 128:(2 * h + 2) * 128,
                              p * CH:(p + 1) * CH].rearrange(
                                  "(hf q) a -> q hf a", q=128),
                     reads=[("qsp", 2 * h, 2 * h + 2)], writes=[("qsh", 0, 2 * CH)],
                     skey=("qsh", 0, 1))
            self.dma("sp", yh[:, :].rearrange("q (ac d) -> q ac d", d=HD),
                     t["ysp"][p * CH:(p + 1) * CH,
                              h * HD:(h + 1) * HD].rearrange(
                                  "(ac q) d -> q ac d", q=128),
                     reads=[("ysp", p * 8 + h, p * 8 + h + 1)],
                     writes=[("yh", 0, NP * HD)], skey=("yh", 0, 1))
            for ac in range(NP):
                pb = ac % 2
                self._mm_acc(pl[pb][:, 0:HD], plk[pb],
                             lambda hf, ac=ac: qsh[:, hf * CH + ac * 128: hf * CH + ac * 128 + 128],
                             lambda hf, h=h: sb["sbf"][:, (2 * h + hf) * HD:(2 * h + hf + 1) * HD],
                             2,
                             lambda hf, ac=ac, h=h: [
                                 ("qsh", hf * CH + ac * 128, hf * CH + ac * 128 + 128),
                                 ("sbf", (2 * h + hf) * HD, (2 * h + hf + 1) * HD)])
                yo = h * NP * HD + ac * HD
                seg = yfull[:, yo:yo + HD]
                segk = ("yfull", yo, yo + HD)
                S.emit("dve", lambda v, pb=pb, ac=ac, seg=seg: v.tensor_add(
                    seg, pl[pb][:, 0:HD], yh[:, ac * HD:(ac + 1) * HD]),
                    reads=[plk[pb], ("yh", ac * HD, (ac + 1) * HD)],
                    writes=[segk])
                st = sb["st"]
                so = (ac % 2) * 8 + 16
                stk = ("st", so, so + 8)
                S.emit("dve", lambda v, seg=seg, so=so: v.reduce_sum(
                    st[:, so:so + 1], seg, axis=AX, negate=True),
                    reads=[segk], writes=[stk])
                S.emit("act", lambda a, seg=seg, so=so: a.activation(
                    sb["sqg"][:, :], seg, AF.Square,
                    accum_out=st[:, so + 1:so + 2]),
                    reads=[segk], writes=[stk, "sqg"])
                S.emit("dve", lambda v, so=so: v.tensor_scalar(
                    st[:, so:so + 1], st[:, so:so + 1], 1.0 / HD, None, ALU.mult),
                    reads=[stk], writes=[stk])
                S.emit("dve", lambda v, so=so: v.tensor_tensor(
                    st[:, so + 2:so + 3], st[:, so:so + 1], st[:, so:so + 1],
                    ALU.mult), reads=[stk], writes=[stk])
                S.emit("dve", lambda v, so=so: v.scalar_tensor_tensor(
                    st[:, so + 2:so + 3], st[:, so + 1:so + 2], 1.0 / HD,
                    st[:, so + 2:so + 3], ALU.mult, ALU.subtract),
                    reads=[stk], writes=[stk])
                S.emit("act", lambda a, so=so: a.activation(
                    st[:, so + 2:so + 3], st[:, so + 2:so + 3], AF.Sqrt,
                    bias=float(EPS)), reads=[stk, "cstf"], writes=[stk])
                S.emit("dve", lambda v, so=so: v.reciprocal(
                    st[:, so + 3:so + 4], st[:, so + 2:so + 3]),
                    reads=[stk], writes=[stk])
                S.emit("dve", lambda v, seg=seg, so=so: v.tensor_scalar(
                    seg, seg, st[:, so:so + 1], st[:, so + 3:so + 4],
                    ALU.add, ALU.mult), reads=[segk, stk], writes=[segk])
                S.emit("dve", lambda v, seg=seg, h=h: v.tensor_mul(
                    seg, seg, sb["gngl"][:, h * HD:(h + 1) * HD]),
                    reads=[segk, "gngl"], writes=[segk])
                S.emit("dve", lambda v, seg=seg, h=h: v.tensor_add(
                    seg, seg, sb["gnbl"][:, h * HD:(h + 1) * HD]),
                    reads=[segk, "gnbl"], writes=[segk])
        for h in range(HEADS):
            thunks.append(lambda h=h: head_th(h))
        return thunks

    def attn_phase2_stage2(self, l, p, pending=()):
        pending = list(pending)
        S, sb, t = self.S, self.sb, self.t
        fmA, wsl = sb["xnT"], sb["wsl"]
        pp = [self.ps["pp0"], self.ps["pp1"]]
        ppk = ["pp0", "pp1"]
        py = [self.ps["py0"], self.ps["py1"]]
        pyk = ["py0", "py1"]
        pt = [self.ps["pt0"], self.ps["pt1"]]
        ptk = ["pt0", "pt1"]
        yfull = sb["yfull"]

        for h in range(HEADS):
            gh = sb["gh"]
            self.dma("sp", gh[:, :].rearrange("q (hf a) -> q hf a", a=CH),
                     t["gsp"][2 * h * 128:(2 * h + 2) * 128,
                              p * CH:(p + 1) * CH].rearrange(
                                  "(hf q) a -> q hf a", q=128),
                     reads=[("gsp", 2 * h, 2 * h + 2)], writes=[("gh", 0, 2 * CH)],
                     skey=("gh", 0, 1))
            pb = h % 2
            for hf in range(2):
                for ac in range(NP):
                    i = hf * NP + ac
                    pslice = pt[pb][:, i * 128:(i + 1) * 128]
                    yo = h * NP * HD + ac * HD + hf * 128
                    self._transpose(pslice, ptk[pb],
                                    yfull[:, yo:yo + 128],
                                    [("yfull", yo, yo + 128)])
            d0 = 2 * h * CH
            S.emit("dve", lambda v, pb=pb, d0=d0: v.tensor_mul(
                fmA[:, d0:d0 + 2 * CH], pt[pb][:, 0:1024], gh[:, 0:2 * CH]),
                reads=[ptk[pb], ("gh", 0, 2 * CH)],
                writes=[("xnT", d0, d0 + 2 * CH)])

        retaF = sb["retaF"]
        for tc4 in range(4):
            tcg = p * 4 + tc4
            self.dma("sp", retaF[:, tc4 * H:(tc4 + 1) * H],
                     t["zd"][tcg * 128:(tcg + 1) * 128, :],
                     reads=[("zd", tcg, tcg + 1)],
                     writes=[("retaF", tc4 * H, (tc4 + 1) * H)],
                     skey=("retaF", tc4 * H, tc4 * H + 1))
        for nc_ in range(KC):
            rot = nc_ % 2
            wslh, wo = self._slab("wob", l, nc_, rot)
            pb = nc_ % 2
            self._mm_acc(pp[pb][:, 0:CH], ppk[pb],
                         lambda kc, wo=wo: wsl[:, wo + kc * 128: wo + (kc + 1) * 128],
                         lambda kc: fmA[:, kc * CH:(kc + 1) * CH],
                         KC,
                         lambda kc, wo=wo: [("wsl", wo + kc * 128, wo + (kc + 1) * 128),
                                            ("xnT", kc * CH, (kc + 1) * CH)])
            ev = sb["ev"]
            eo = rot * CH
            S.emit("act", lambda a, pb=pb, eo=eo: a.copy(
                ev[:, eo:eo + CH], pp[pb][:, 0:CH]),
                reads=[ppk[pb]], writes=[("ev", eo, eo + CH)])
            pb2 = nc_ % 2
            for tc4 in range(4):
                pslice = pt[pb2][:, tc4 * 128:(tc4 + 1) * 128]
                self._transpose(pslice, ptk[pb2],
                                sb["ev"][:, eo + tc4 * 128: eo + (tc4 + 1) * 128],
                                [("ev", eo + tc4 * 128, eo + (tc4 + 1) * 128)])
            rv = retaF[:, :].rearrange("q (tc4 nc c) -> q tc4 nc c",
                                       tc4=4, c=128)[:, :, nc_, :]
            rkeys = [("retaF", t4 * H + nc_ * 128, t4 * H + (nc_ + 1) * 128)
                     for t4 in range(4)]
            S.emit("dve", lambda v, pb2=pb2, rv=rv: v.tensor_add(
                rv, rv, pt[pb2][:, 0:512].rearrange("q (tc4 c) -> q tc4 c", c=128)),
                reads=[ptk[pb2]] + rkeys, writes=rkeys)

        for tc4 in range(4):
            r = tc4 * H
            so = (tc4 % 2) * 8 + 32
            self._ln_stats(retaF, r, "retaF", so)
            S.emit("dve", lambda v, r=r, tc4=tc4, so=so: v.tensor_scalar(
                sb["xnbf"][:, (tc4 % 2) * H:(tc4 % 2) * H + H],
                retaF[:, r:r + H], sb["st"][:, so:so + 1],
                sb["st"][:, so + 3:so + 4], ALU.add, ALU.mult),
                reads=[("retaF", r, r + H), ("st", so, so + 8)],
                writes=[("xnbf", (tc4 % 2) * H, (tc4 % 2) * H + H)])
            for kc in range(KC):
                i = tc4 * KC + kc
                pb = i % 2
                po = (i // 2 % 4) * 128
                pslice = pt[pb][:, po:po + 128]
                rr = (tc4 % 2) * H
                self._transpose(pslice, ptk[pb],
                                sb["xnbf"][:, rr + kc * 128: rr + (kc + 1) * 128],
                                [("xnbf", rr + kc * 128, rr + (kc + 1) * 128)])
                d = kc * CH + tc4 * 128
                S.emit("dve", lambda v, ps=pslice, d=d, kc=kc: v.tensor_scalar(
                    fmA[:, d:d + 128], ps, sb["lng"][:, kc:kc + 1],
                    sb["lnb"][:, kc:kc + 1], ALU.mult, ALU.add),
                    reads=[ptk[pb], "lng", "lnb"],
                    writes=[("xnT", d, d + 128)])

        while pending:
            pending.pop(0)()
        h1g = sb["h1g"]
        for grp in range(4):
            for fs in range(KC):
                f = grp * KC + fs
                rot = fs % 2
                wslh, wo = self._slab("w1b", l, f, rot)
                pb = fs % 2
                self._mm_acc(pp[pb][:, 0:CH], ppk[pb],
                             lambda kc, wo=wo: wsl[:, wo + kc * 128: wo + (kc + 1) * 128],
                             lambda kc: fmA[:, kc * CH:(kc + 1) * CH],
                             KC,
                             lambda kc, wo=wo: [("wsl", wo + kc * 128, wo + (kc + 1) * 128),
                                                ("xnT", kc * CH, (kc + 1) * CH)])
                S.emit("act", lambda a, pb=pb, fs=fs, f=f, l=l: a.activation(
                    h1g[:, fs * CH:(fs + 1) * CH], pp[pb][:, 0:CH], AF.Gelu,
                    bias=sb["b1c"][:, l * FC + f: l * FC + f + 1]),
                    reads=[ppk[pb], "b1c"], writes=[("h1g", fs * CH, (fs + 1) * CH)])
            for nc_ in range(KC):
                rot = nc_ % 2
                w = self.t["w2b"][l, nc_][:, grp * H:(grp + 1) * H]
                dst = wsl[:, rot * H: rot * H + H]
                self.dma("sp", dst, w, reads=[("w2b", l * 64 + nc_, l * 64 + nc_ + 1)],
                         writes=[("wsl", rot * H, rot * H + H)],
                         skey=("wsl", rot * H, rot * H + 1))
                wo = rot * H
                pb = nc_ % 2
                self._mm_acc(py[pb][:, 0:CH], pyk[pb],
                             lambda fs, wo=wo: wsl[:, wo + fs * 128: wo + (fs + 1) * 128],
                             lambda fs: h1g[:, fs * CH:(fs + 1) * CH],
                             KC,
                             lambda fs, wo=wo: [("wsl", wo + fs * 128, wo + (fs + 1) * 128),
                                                ("h1g", fs * CH, (fs + 1) * CH)])
                eo = rot * CH
                if grp == 0:
                    S.emit("act", lambda a, pb=pb, eo=eo, nc_=nc_, l=l: a.add(
                        sb["ev"][:, eo:eo + CH], py[pb][:, 0:CH],
                        sb["b2c"][:, l * KC + nc_: l * KC + nc_ + 1]),
                        reads=[pyk[pb], "b2c"], writes=[("ev", eo, eo + CH)])
                else:
                    S.emit("act", lambda a, pb=pb, eo=eo: a.copy(
                        sb["ev"][:, eo:eo + CH], py[pb][:, 0:CH]),
                        reads=[pyk[pb]], writes=[("ev", eo, eo + CH)])
                pb2 = nc_ % 2
                for tc4 in range(4):
                    pslice = pt[pb2][:, tc4 * 128:(tc4 + 1) * 128]
                    self._transpose(pslice, ptk[pb2],
                                    sb["ev"][:, eo + tc4 * 128: eo + (tc4 + 1) * 128],
                                    [("ev", eo + tc4 * 128, eo + (tc4 + 1) * 128)])
                rv = retaF[:, :].rearrange("q (tc4 nc c) -> q tc4 nc c",
                                           tc4=4, c=128)[:, :, nc_, :]
                rkeys = [("retaF", t4 * H + nc_ * 128, t4 * H + (nc_ + 1) * 128)
                         for t4 in range(4)]
                S.emit("dve", lambda v, pb2=pb2, rv=rv: v.tensor_add(
                    rv, rv, pt[pb2][:, 0:512].rearrange("q (tc4 c) -> q tc4 c", c=128)),
                    reads=[ptk[pb2]] + rkeys, writes=rkeys)
        while pending:
            pending.pop(0)()
        for tc4 in range(4):
            tcg = p * 4 + tc4
            self.dma("act", t["zd"][tcg * 128:(tcg + 1) * 128, :],
                     retaF[:, tc4 * H:(tc4 + 1) * H],
                     reads=[("retaF", tc4 * H, (tc4 + 1) * H)],
                     writes=[("zd", tcg, tcg + 1)],
                     skey=("retaF", tc4 * H, tc4 * H + 1))

    def _ln_stats(self, buf, r, key, so, n=H):
        sb, S = self.sb, self.S
        st, sq = sb["st"], sb["sq"]
        bk = (key, r, r + n)
        stk = ("st", so, so + 8)
        S.emit("dve", lambda v: v.reduce_sum(st[:, so:so + 1], buf[:, r:r + n],
                                             axis=AX, negate=True),
               reads=[bk], writes=[stk])
        S.emit("act", lambda a: a.activation(sq[:, 0:n], buf[:, r:r + n], AF.Square,
                                             accum_out=st[:, so + 1:so + 2]),
               reads=[bk], writes=[stk, ("sq", 0, n)])
        S.emit("dve", lambda v: v.tensor_scalar(st[:, so:so + 1], st[:, so:so + 1],
                                                1.0 / n, None, ALU.mult),
               reads=[stk], writes=[stk])
        S.emit("dve", lambda v: v.tensor_tensor(st[:, so + 2:so + 3],
                                                st[:, so:so + 1], st[:, so:so + 1],
                                                ALU.mult), reads=[stk], writes=[stk])
        S.emit("dve", lambda v: v.scalar_tensor_tensor(st[:, so + 2:so + 3],
                                                       st[:, so + 1:so + 2], 1.0 / n,
                                                       st[:, so + 2:so + 3],
                                                       ALU.mult, ALU.subtract),
               reads=[stk], writes=[stk])
        S.emit("act", lambda a: a.activation(st[:, so + 2:so + 3],
                                             st[:, so + 2:so + 3], AF.Sqrt,
                                             bias=float(EPS)),
               reads=[stk, "cstf"], writes=[stk])
        S.emit("dve", lambda v: v.reciprocal(st[:, so + 3:so + 4],
                                             st[:, so + 2:so + 3]),
               reads=[stk], writes=[stk])

    def attn_layer(self, l, post_pass=None):
        S, sb, t = self.S, self.sb, self.t
        self.dma("sp", sb["gngl"][:, :], t["gng"][l], writes=["gngl"],
                 skey=("gngl", 0, 1))
        self.dma("sp", sb["gnbl"][:, :], t["gnb"][l], writes=["gnbl"],
                 skey=("gnbl", 0, 1))
        S.emit("dve", lambda v: v.memset(sb["sst"][:, :], 0.0), writes=["sst"])
        for p in range(NP):
            self.attn_phase1(l, p)
        self.state_exchange(l)
        for th in self.ph2_stage1_thunks(l, 0):
            th()
        for p in range(NP):
            nxt = self.ph2_stage1_thunks(l, p + 1) if p + 1 < NP else []
            self.attn_phase2_stage2(l, p, pending=nxt)
            if post_pass is not None:
                post_pass(p)

    def alloc_moe(self):
        c = self._glob_end
        c = self.view("wguA", c, KC * 768, BF)
        c = self.view("wguB", c, KC * 768, BF)
        c = self.view("wdA", c, 3 * H, BF)
        c = self.view("wdB", c, 3 * H, BF)
        c = self.view("xft", c, KC * CH, BF)
        c = self.view("xft2", c, KC * CH, BF)
        c = self.view("act", c, 3 * CH, BF)
        c = self.view("yblk", c, 4 * H, BF)
        c = self.view("dqw", c, 2048, I32)
        c = self.view("dq1", c, 2048, I32)
        c = self.view("dq2", c, 2048, F32)
        c = self.view("dq3", c, 2048, F32)
        c = self.view("dqs", c, 2048, F32)
        c = self.view("cbc", c, 32 * 4 * EPER, F32)
        c = self.view("cbr", c, 16, F32)
        self._moe_end = c


    def x1_xf_stage(self, p=None):
        S, sb, t = self.S, self.sb, self.t
        pp = [self.ps["pp0"], self.ps["pp1"]]
        ppk = ["pp0", "pp1"]
        pt = [self.ps["pt0"], self.ps["pt1"]]
        ptk = ["pt0", "pt1"]
        pl = self.ps["pl0"]
        rng = range(TC // 128) if p is None else range(p * 4, (p + 1) * 4)
        for tc in rng:
            r = (tc % 2) * H
            zt = sb["zt"]
            self.dma("sp", zt[:, r:r + H], t["zd"][tc * 128:(tc + 1) * 128, :],
                     reads=[("zd", tc, tc + 1)], writes=[("zt", r, r + H)],
                     skey=("zt", r, r + H))
            self.dma("sp", sb["sq"][:, :], t["x0"][tc * 128:(tc + 1) * 128, :],
                     writes=[("sq", 0, H)], skey=("sq", 0, 1))
            S.emit("dve", lambda v, r=r: v.tensor_add(zt[:, r:r + H],
                                                      zt[:, r:r + H],
                                                      sb["sq"][:, :]),
                   reads=[("zt", r, r + H), ("sq", 0, H)],
                   writes=[("zt", r, r + H)])
            self.dma("sp", t["x1d"][tc * 128:(tc + 1) * 128, :], zt[:, r:r + H],
                     reads=[("zt", r, r + H)], writes=[("x1d", tc, tc + 1)],
                     skey=("zt", r, r + H))
            so = (tc % 2) * 8
            self._ln_stats(zt, r, "zt", so)
            S.emit("dve", lambda v, r=r, so=so: v.tensor_scalar(
                sb["xnbf"][:, r:r + H], zt[:, r:r + H], sb["st"][:, so:so + 1],
                sb["st"][:, so + 3:so + 4], ALU.add, ALU.mult),
                reads=[("zt", r, r + H), ("st", so, so + 8)],
                writes=[("xnbf", r, r + H)])
            S.emit("dve", lambda v, r=r, so=so: v.tensor_scalar(
                zt[:, r:r + H], zt[:, r:r + H], sb["st"][:, so:so + 1],
                sb["st"][:, so + 3:so + 4], ALU.add, ALU.mult),
                reads=[("zt", r, r + H), ("st", so, so + 8)],
                writes=[("zt", r, r + H)])
            for kc in range(KC):
                i = tc * KC + kc
                pb = i % 2
                po = (i // 2 % 4) * 128
                pslice = pt[pb][:, po:po + 128]
                self._transpose(pslice, ptk[pb],
                                sb["xnbf"][:, r + kc * 128: r + (kc + 1) * 128],
                                [("xnbf", r + kc * 128, r + (kc + 1) * 128)])
                d = kc * 128
                S.emit("act", lambda a, ps=pslice, d=d: a.copy(
                    sb["xftS"][:, d:d + 128], ps),
                    reads=[ptk[pb]],
                    writes=[("xftS", d, d + 128)])
                pb2 = kc % 2
                self._transpose(pp[pb2][:, 0:128], (ppk[pb2], 0, 128),
                                zt[:, r + kc * 128: r + (kc + 1) * 128],
                                [("zt", r + kc * 128, r + (kc + 1) * 128)],
                                f32=True)
                S.emit("act", lambda a, pb2=pb2, kc=kc: a.copy(
                    sb["xft32"][:, kc * 128:(kc + 1) * 128], pp[pb2][:, 0:128]),
                    reads=[(ppk[pb2], 0, 128)],
                    writes=[("xft32", kc * 128, (kc + 1) * 128)])
            self.dma("act", t["xfin"][:, tc * 128:(tc + 1) * 128].rearrange(
                "(kc q) c -> q kc c", q=128),
                sb["xftS"][:, 0:KC * 128].rearrange("q (kc c) -> q kc c", c=128),
                reads=[("xftS", 0, KC * 128)], writes=[("xfin", tc, tc + 1)],
                skey=("xftS", 0, 1))
            self._mm_acc(pl[:, 0:16], "pl0",
                         lambda kc: sb["xft32"][:, kc * 128:(kc + 1) * 128],
                         lambda kc: sb["gwt"][:, kc * E:(kc + 1) * E],
                         KC,
                         lambda kc: [("xft32", kc * 128, (kc + 1) * 128), "gwt"])
            S.emit("dve", lambda v: v.tensor_copy(sb["cw"][:, 0:16], pl[:, 0:16]),
                   reads=["pl0"], writes=[("cw", 0, 16)])
            self.dma("act", t["cmbin"][tc * 128:(tc + 1) * 128, :], sb["cw"][:, 0:16],
                     reads=[("cw", 0, 16)], writes=[("cmbin", tc, tc + 1)],
                     skey=("cw", 0, 1))

    def moe_exchange(self):
        S, t = self.S, self.t
        rg = [list(range(NC))]
        S.emit("pool", lambda g: g.collective_compute(
            "AllGather", ALU.bypass, rg, [t["xfin"].opt()], [t["xfout"].opt()]),
            reads=[("xfin", 0, TC)], writes=["xfout"], cc=True)
        S.emit("pool", lambda g: g.collective_compute(
            "AllGather", ALU.bypass, rg, [t["cmbin"].opt()], [t["cmbout"].opt()]),
            reads=["cmbin"], writes=["cmbout"], cc=True)

    def combine_tb(self, tb):
        S, sb, t = self.S, self.sb, self.t
        gr = tb * 512
        self.dma("sp", sb["cw"][:, :].rearrange("q (tcl e) -> q tcl e", e=E),
                 t["cmbout"][gr:gr + 512, :].rearrange("(tcl q) e -> q tcl e", q=128),
                 reads=["cmbout"], writes=[("cw", 0, 64)], skey=("cw", 0, 1))
        for tcl in range(4):
            self._softmax_top4(co=tcl * 16)
            for e in range(EPER):
                col = (tb * 4 + tcl) * EPER + e
                S.emit("dve", lambda v, e=e: v.tensor_mul(
                    sb["mk"][:, :], sb["cmb"][:, :],
                    sb["esel"][:, e * E:(e + 1) * E]),
                    reads=["cmb", "esel"], writes=["mk"])
                S.emit("dve", lambda v, col=col: v.reduce_sum(
                    sb["cbc"][:, col:col + 1], sb["mk"][:, :], axis=AX),
                    reads=["mk"], writes=[("cbc", col, col + 1)])

    def _softmax_top4(self, co=0):
        S, sb = self.S, self.sb
        st = sb["st"]
        cwk = ("cw", co, co + 16)
        cws = sb["cw"][:, co:co + 16]
        S.emit("dve", lambda v: v.reduce_max(st[:, 48:49], cws, axis=AX,
                                             negate=True),
               reads=[cwk], writes=[("st", 48, 56)])
        S.emit("act", lambda a: a.activation(cws, cws, AF.Exp,
                                             bias=st[:, 48:49]),
               reads=[cwk, ("st", 48, 56)], writes=[cwk])
        S.emit("dve", lambda v: v.reduce_sum(st[:, 49:50], cws, axis=AX),
               reads=[cwk], writes=[("st", 48, 56)])
        S.emit("dve", lambda v: v.reciprocal(st[:, 50:51], st[:, 49:50]),
               reads=[("st", 48, 56)], writes=[("st", 48, 56)])
        S.emit("dve", lambda v: v.tensor_scalar(cws, cws,
                                                st[:, 50:51], None, ALU.mult),
               reads=[cwk, ("st", 48, 56)], writes=[cwk])
        S.emit("dve", lambda v: v.max(sb["m8"][:, :], cws),
               reads=[cwk], writes=["m8"])
        S.emit("dve", lambda v: v.tensor_scalar(sb["mk"][:, :], cws,
                                                sb["m8"][:, 3:4], None, ALU.is_ge),
               reads=[cwk, "m8"], writes=["mk"])
        S.emit("dve", lambda v: v.tensor_mul(sb["cmb"][:, :], sb["mk"][:, :],
                                             cws),
               reads=["mk", cwk], writes=["cmb"])
        S.emit("dve", lambda v: v.reduce_sum(st[:, 51:52], sb["cmb"][:, :], axis=AX),
               reads=["cmb"], writes=[("st", 48, 56)])
        S.emit("dve", lambda v: v.reciprocal(st[:, 52:53], st[:, 51:52]),
               reads=[("st", 48, 56)], writes=[("st", 48, 56)])
        S.emit("dve", lambda v: v.tensor_scalar(sb["cmb"][:, :], sb["cmb"][:, :],
                                                st[:, 52:53], None, ALU.mult),
               reads=["cmb", ("st", 48, 56)], writes=["cmb"])

    def dma_acc(self, eng, out, in_, accum, reads=(), writes=(), skey=None):
        op = ALU.add if accum else ALU.bypass
        self.S.emit(eng, lambda e, o=out, i=in_, op=op: e.dma_start(
            out=o, in_=i, accum_op=op), reads=reads, writes=writes, dma=True,
            skey=skey)

    def _dequant_tile(self, eng, wkey, wdst, width, scale_reads):
        S, sb = self.S, self.sb
        dq1i = sb["dq1"]
        dq1f = dq1i.bitcast(F32)
        dq2, dq3, dqs = sb["dq2"], sb["dq3"], sb["dqs"]
        w = width
        ks = [("dqw", 0, w), ("dq1", 0, w), ("dq2", 0, w), ("dq3", 0, w)]

        def em(fn, rd, wr):
            S.emit(eng, fn, reads=rd, writes=wr)

        em(lambda v: v.tensor_scalar(dq1i[:, 0:w], sb["dqw"][:, 0:w],
                                     sb["m15c"][:, :], None, ALU.bitwise_and),
           [("dqw", 0, w), "m15c"], [("dq1", 0, w)])
        em(lambda v: v.tensor_copy(dq2[:, 0:w], dq1i[:, 0:w]),
           [("dq1", 0, w)], [("dq2", 0, w)])
        em(lambda v: v.tensor_scalar(dq2[:, 0:w], dq2[:, 0:w], sb["p2nc"][:, :],
                                     None, ALU.mult),
           [("dq2", 0, w), "p2nc"], [("dq2", 0, w)])
        em(lambda v: v.tensor_scalar(dq3[:, 0:w], dq2[:, 0:w], 8.0, 8.0,
                                     ALU.is_ge, ALU.mult),
           [("dq2", 0, w)], [("dq3", 0, w)])
        em(lambda v: v.tensor_sub(dq2[:, 0:w], dq2[:, 0:w], dq3[:, 0:w]),
           [("dq2", 0, w), ("dq3", 0, w)], [("dq2", 0, w)])
        em(lambda v: v.tensor_scalar(dq3[:, 0:w], dq3[:, 0:w], -0.25, 1.0,
                                     ALU.mult, ALU.add),
           [("dq3", 0, w)], [("dq3", 0, w)])
        em(lambda v: v.tensor_copy(dq1i[:, 0:w], dq2[:, 0:w]),
           [("dq2", 0, w)], [("dq1", 0, w)])
        em(lambda v: v.tensor_scalar(dq1i[:, 0:w], dq1i[:, 0:w], 22, None,
                                     ALU.logical_shift_left),
           [("dq1", 0, w)], [("dq1", 0, w)])
        em(lambda v: v.tensor_scalar(dq1i[:, 0:w], dq1i[:, 0:w], 0x3F000000, None,
                                     ALU.add),
           [("dq1", 0, w)], [("dq1", 0, w)])
        em(lambda v: v.tensor_scalar(dq2[:, 0:w], dq1f[:, 0:w], -1.0, 0.0,
                                     ALU.add, ALU.min),
           [("dq1", 0, w)], [("dq2", 0, w)])
        em(lambda v: v.tensor_add(dq2[:, 0:w], dq1f[:, 0:w], dq2[:, 0:w]),
           [("dq1", 0, w), ("dq2", 0, w)], [("dq2", 0, w)])
        em(lambda v: v.tensor_mul(dq2[:, 0:w], dq2[:, 0:w], dq3[:, 0:w]),
           [("dq2", 0, w), ("dq3", 0, w)], [("dq2", 0, w)])
        em(lambda v, wdst=wdst: v.tensor_mul(wdst, dq2[:, 0:w], dqs[:, 0:w]),
           [("dq2", 0, w), ("dqs", 0, w)] + list(scale_reads), [wkey])

    def _vexp_list(self):
        lst = [dict(kind="exp", e=e, q=q) for e in range(EPER) for q in range(4)]
        lst.append(dict(kind="sh"))
        return lst

    def dequant_thunks(self, ve, buf):
        thunks = []
        t = self.t
        wgu, wd = "wgu" + buf, "wd" + buf
        if ve["kind"] == "exp":
            e, q = ve["e"], ve["q"]
            for kc in range(KC):
                def th(kc=kc, e=e, q=q):
                    g0 = q * 384
                    u0 = IFF + q * 384
                    rows = slice(kc * 128, (kc + 1) * 128)
                    self.dma("sp", self.sb["dqw"][:, 0:384],
                             t["gupw"][e][rows, g0:g0 + 384],
                             writes=[("dqw", 0, 384)], skey=("dqw", 0, 384))
                    self.dma("sp", self.sb["dqw"][:, 384:768],
                             t["gupw"][e][rows, u0:u0 + 384],
                             writes=[("dqw", 384, 768)], skey=("dqw", 384, 768))
                    self.dma("sp", self.sb["dqs"][:, 0:384],
                             t["gusr"][e][rows, g0:g0 + 384],
                             writes=[("dqs", 0, 384)], skey=("dqs", 0, 384))
                    self.dma("sp", self.sb["dqs"][:, 384:768],
                             t["gusr"][e][rows, u0:u0 + 384],
                             writes=[("dqs", 384, 768)], skey=("dqs", 384, 768))
                    self._dequant_tile("dve", (wgu, kc * 768, (kc + 1) * 768),
                                       self.sb[wgu][:, kc * 768:(kc + 1) * 768],
                                       768, [])
                thunks.append(th)
            for ic in range(3):
                def th(ic=ic, e=e, q=q):
                    r0 = q * 384 + ic * 128
                    self.dma("sp", self.sb["dqw"][:, 0:2048],
                             t["dnpw"][e][r0:r0 + 128, :],
                             writes=[("dqw", 0, 2048)], skey=("dqw", 0, 384))
                    self.dma("sp", self.sb["dqs"][:, 0:2048],
                             t["dnsr"][e][r0:r0 + 128, :],
                             writes=[("dqs", 0, 2048)], skey=("dqs", 0, 384))
                    self._dequant_tile("dve", (wd, ic * H, (ic + 1) * H),
                                       self.sb[wd][:, ic * H:(ic + 1) * H], 2048, [])
                thunks.append(th)
        else:
            for kc in range(KC):
                def th(kc=kc):
                    rows = slice(kc * 128, (kc + 1) * 128)
                    self.dma("sp", self.sb["dqw"][:, 0:384], t["sgupw"][rows, :],
                             writes=[("dqw", 0, 384)], skey=("dqw", 0, 384))
                    self.dma("sp", self.sb["dqs"][:, 0:384], t["sgusr"][rows, :],
                             writes=[("dqs", 0, 384)], skey=("dqs", 0, 384))
                    self._dequant_tile("dve", (wgu, kc * 768, kc * 768 + 384),
                                       self.sb[wgu][:, kc * 768:kc * 768 + 384],
                                       384, [])
                thunks.append(th)
            for ic, (r0, rn) in enumerate(((0, 128), (128, 64))):
                def th(ic=ic, r0=r0, rn=rn):
                    self.dma("sp", self.sb["dqw"][0:rn, 0:2048],
                             t["sdnpw"][r0:r0 + rn, :],
                             writes=[("dqw", 0, 2048)], skey=("dqw", 0, 384))
                    self.dma("sp", self.sb["dqs"][0:rn, 0:2048],
                             t["sdnsr"][r0:r0 + rn, :],
                             writes=[("dqs", 0, 2048)], skey=("dqs", 0, 384))
                    self._dequant_tile("dve", (wd, ic * H, (ic + 1) * H),
                                       self.sb[wd][:, ic * H:(ic + 1) * H], 2048, [])
                thunks.append(th)
        return thunks

    def ve_matmuls(self, ve, buf, first, pending=()):
        pending = list(pending)
        S, sb, t = self.S, self.sb, self.t
        wgu, wd = sb["wgu" + buf], sb["wd" + buf]
        wguk, wdk = "wgu" + buf, "wd" + buf
        pp = [self.ps["pp0"], self.ps["pp1"]]
        ppk = ["pp0", "pp1"]
        py = [self.ps["py0"], self.ps["py1"]]
        pyk = ["py0", "py1"]
        sh = ve["kind"] == "sh"
        if sh:
            pchunks = [(0, 128), (128, 64)]
            uoff = 192
        else:
            pchunks = [(0, 128), (128, 128), (256, 128)]
            uoff = 384
        for tb in range(32):
            if first:
                if tb == 0:
                    self.combine_tb(0)
                    self.combine_tb(1)
                elif tb + 1 < 32:
                    self.combine_tb(tb + 1)
            if tb >= 4 and pending:
                n_each = max(1, (len(pending) + 27) // 28)
                for _ in range(n_each):
                    if pending:
                        pending.pop(0)()
            r0, c0 = tb // 4, (tb % 4) * CH
            xfn = "xft" if tb % 2 == 0 else "xft2"
            xft = sb[xfn]
            self.dma("sp", xft[:, :].rearrange("q (kc t) -> q kc t", t=CH),
                     t["xfout"][r0 * H:(r0 + 1) * H, c0:c0 + CH].rearrange(
                         "(kc q) t -> q kc t", q=128),
                     reads=["xfout"], writes=[(xfn, 0, KC * CH)],
                     skey=(xfn, 0, 1))
            for pi, (pc0, pw) in enumerate(pchunks):
                bp = [pp, py][pi % 2]
                bpk = [ppk, pyk][pi % 2]
                for gu in range(2):
                    off = pc0 + gu * uoff
                    pb = gu
                    self._mm_acc(bp[pb][0:pw, 0:CH], bpk[pb],
                                 lambda kc, off=off, pw=pw: wgu[:, kc * 768 + off:
                                                                kc * 768 + off + pw],
                                 lambda kc, xft=xft: xft[:, kc * CH:(kc + 1) * CH],
                                 KC,
                                 lambda kc, off=off, pw=pw, xfn=xfn: [
                                     (wguk, kc * 768 + off, kc * 768 + off + pw),
                                     (xfn, kc * CH, (kc + 1) * CH)])
                eo2 = (pi % 2) * CH
                S.emit("act", lambda a, bp=bp, eo2=eo2: a.activation(
                    sb["ev"][:, eo2:eo2 + CH], bp[0][:, 0:CH], AF.Silu),
                    reads=[bpk[0]], writes=[("ev", eo2, eo2 + CH)])
                S.emit("dve", lambda v, pi=pi, bp=bp, eo2=eo2: v.tensor_mul(
                    sb["act"][:, pi * CH:(pi + 1) * CH], sb["ev"][:, eo2:eo2 + CH],
                    bp[1][:, 0:CH]),
                    reads=[("ev", eo2, eo2 + CH), bpk[1]],
                    writes=[("act", pi * CH, (pi + 1) * CH)])
            for tcl in range(4):
                for n2 in range(4):
                    pb = (tcl * 4 + n2) % 2
                    self._mm_acc(
                        py[pb][:, 0:CH], pyk[pb],
                        lambda ic, tcl=tcl: sb["act"][0:pchunks[ic][1],
                                                      ic * CH + tcl * 128:
                                                      ic * CH + tcl * 128 + 128],
                        lambda ic, n2=n2: wd[0:pchunks[ic][1],
                                             ic * H + n2 * CH: ic * H + (n2 + 1) * CH],
                        len(pchunks),
                        lambda ic, tcl=tcl, n2=n2: [
                            ("act", ic * CH + tcl * 128, ic * CH + tcl * 128 + 128),
                            (wdk, ic * H + n2 * CH, ic * H + (n2 + 1) * CH)])
                    yo = tcl * H + n2 * CH
                    if sh:
                        S.emit("dve", lambda v, pb=pb, yo=yo: v.tensor_copy(
                            sb["yblk"][:, yo:yo + CH], py[pb][:, 0:CH]),
                            reads=[pyk[pb]], writes=[("yblk", yo, yo + CH)])
                    else:
                        col = (tb * 4 + tcl) * EPER + ve["e"]
                        S.emit("dve", lambda v, pb=pb, yo=yo, col=col: v.tensor_scalar(
                            sb["yblk"][:, yo:yo + CH], py[pb][:, 0:CH],
                            sb["cbc"][:, col:col + 1], None, ALU.mult),
                            reads=[pyk[pb], ("cbc", col, col + 1)],
                            writes=[("yblk", yo, yo + CH)])
            self.dma_acc("pool",
                         t["part"][tb * CH:(tb + 1) * CH, :].rearrange(
                             "(tcl q) n -> q tcl n", q=128),
                         sb["yblk"][:, :].rearrange("q (tcl n) -> q tcl n", n=H),
                         accum=not first,
                         reads=[("yblk", 0, 4 * H)],
                         writes=[("part", tb, tb + 1)],
                         skey=("yblk", 0, 1))

    def moe_compute(self, first_dequanted=False):
        ves = self._vexp_list()
        if not first_dequanted:
            for th in self.dequant_thunks(ves[0], "A"):
                th()
        for vi, ve in enumerate(ves):
            buf = "A" if vi % 2 == 0 else "B"
            nxt = (self.dequant_thunks(ves[vi + 1], "B" if buf == "A" else "A")
                   if vi + 1 < len(ves) else [])
            self.ve_matmuls(ve, buf, first=(vi == 0), pending=nxt)

    def final(self):
        S, sb, t = self.S, self.sb, self.t
        rg = [list(range(NC))]
        S.emit("pool", lambda g: g.collective_compute(
            "ReduceScatter", ALU.add, rg, [t["part"].opt()], [t["rsout"].opt()]),
            reads=[("part", 0, 32)], writes=["rsout"], cc=True)
        for tc in range(TC // 128):
            r = (tc % 2) * H
            self.dma("sp", sb["zt"][:, r:r + H], t["x1d"][tc * 128:(tc + 1) * 128, :],
                     reads=[("x1d", tc, tc + 1)], writes=[("zt", r, r + H)],
                     skey=("zt", r, r + H))
            self.dma("sp", sb["xnbf"][:, r:r + H], t["rsout"][tc * 128:(tc + 1) * 128, :],
                     reads=["rsout"], writes=[("xnbf", r, r + H)],
                     skey=("xnbf", r, r + H))
            S.emit("dve", lambda v, r=r: v.tensor_add(
                sb["zt"][:, r:r + H], sb["zt"][:, r:r + H], sb["xnbf"][:, r:r + H]),
                reads=[("zt", r, r + H), ("xnbf", r, r + H)],
                writes=[("zt", r, r + H)])
            self.dma("sp", t["yout"][tc * 128:(tc + 1) * 128, :], sb["zt"][:, r:r + H],
                     reads=[("zt", r, r + H)], writes=[("yout", tc, tc + 1)],
                     skey=("zt", r, r + H))
        self.S.wait_only("sp", reads=[("yout", 0, TC // 128)])

    def build(self):
        nc = self.nc
        self.make_arena()
        self.load_consts()
        self.precast()
        self.alloc_attn()
        self.attn_layer(0)
        self.tap2("zl0", self.t["zd"], "zd")
        self.alloc_moe()
        self.attn_layer(1, post_pass=lambda p: self.x1_xf_stage(p))
        self.tap2("zl1", self.t["zd"], "zd")
        self.tap2("x1", self.t["x1d"], "x1d")
        self.moe_exchange()
        ves = self._vexp_list()
        for th in self.dequant_thunks(ves[0], "A"):
            th()
        self.moe_compute(first_dequanted=True)
        self.final()
        S = self.S
        per = S.prepare()
        trks = list(S.cnt.keys())
        assert len(trks) <= 100, f"too many semaphores: {len(trks)}"
        with contextlib.ExitStack() as es:
            sems = {t_: es.enter_context(nc.semaphore("sem%d" % i_))
                    for i_, t_ in enumerate(trks)}
            blk = es.enter_context(nc.Block())

            @blk.tensor
            def _(eng):
                S.replay_engine("pe", eng, sems, per)

            @blk.scalar
            def _(eng):
                S.replay_engine("act", eng, sems, per)

            @blk.vector
            def _(eng):
                S.replay_engine("dve", eng, sems, per)

            @blk.gpsimd
            def _(eng):
                S.replay_engine("pool", eng, sems, per)

            @blk.sync
            def _(eng):
                S.replay_engine("sp", eng, sems, per)
        return nc

    def tap2(self, name, src, skey):
        if name not in self.taps:
            return
        dst = self.t["tap_" + name]
        buf = self.sb["zt"]
        for tc in range(TC // 128):
            r = (tc % 2) * H
            self.dma("sp", buf[:, r:r + H], src[tc * 128:(tc + 1) * 128, :],
                     reads=[(skey, tc, tc + 1)], writes=[("zt", r, r + H)],
                     skey=("zt", r, r + H))
            self.dma("sp", dst[tc * 128:(tc + 1) * 128, :], buf[:, r:r + H],
                     reads=[("zt", r, r + H)], writes=[("tap" + name, tc, tc + 1)],
                     skey=("zt", r, r + H))
        self.S.wait_only("sp", reads=[("tap" + name, 0, TC // 128)])


def kernel(**inputs):
    maps = prep_inputs(inputs)
    kb = KB()
    kb.build()
    res = run_bass_kernel_spmd(kb.nc, maps, core_ids=list(range(NC)))
    out = np.concatenate([res.results[c]["yout"] for c in range(NC)], 0)
    return out.reshape(B, S, H)
